# version 28
# baseline (speedup 1.0000x reference)
import contextlib
import numpy as np
import ml_dtypes
import concourse.bass as bass
import concourse.mybir as mybir
from concourse.bass_utils import run_bass_kernel_spmd

F32 = mybir.dt.float32
BF16 = mybir.dt.bfloat16
U8 = mybir.dt.uint8
AF = mybir.ActivationFunctionType
ALU = mybir.AluOpType
AX = mybir.AxisListType

C = 2048
T = 8192
NCORES = 8
R = 1024
HALO = 64
HR = R + 2 * HALO
NH = 16
NKV = 4
HD = 128
KT = 16
INTER = 8192
NI = 64
G = 4
NTR = 8
NTH = 9
NEG = -1.0e30
SCALE = 1.0 / float(np.sqrt(128.0))

ARENA = 209920
O_XNT = 0
O_R1 = 36864
O_R2 = 102400
O_R3 = 126976
O_R4 = 163840
O_R5 = 182272
O_MASKS = O_R5
O_IDENT = O_R5 + 3072
O_SINKB = O_R5 + 3328
O_SMALL = O_R5 + 3392
O_TABQ = O_R5 + 3648
O_TABK = O_R5 + 7744
O_SCR = O_R5 + 12352
O_NAT = O_R5 + 14400
O_SCR2 = O_R5 + 25216
O_WQ1 = O_R5 + 16448
O_ONES = O_R5 + 24640
O_SINK = O_R5 + 25152
O_ESB = O_TABQ
O_PSB = O_SCR
O_PTSB = O_WQ1

SM_S1, SM_S2, SM_M, SM_MSQ, SM_VAR, SM_LNV, SM_RSTD = 0, 4, 8, 12, 16, 20, 24
SM_EPS = 28
SM_DEN, SM_RC = 40, 56

DSZ = {F32: 4, BF16: 2}
SERIAL = False
SERIAL_SET = set()
OWN = {"tensor": "pe", "vector": "dve", "scalar": "act"}


class Prog:

    def __init__(self):
        self.streams = {k: [] for k in ("tensor", "vector", "scalar", "sync", "gpsimd")}
        self.cnt = {}
        self.phase = "A"

    def set_phase(self, name):
        self.phase = name

    def w(self, eng, sem, val):
        if val > 0:
            self.streams[eng].append(("w", sem, val))

    def fence(self, eng):
        own = OWN[eng]
        self.w(eng, own, self.cnt.get(own, 0))

    def op(self, eng, fn, inc=None, by=1):
        if SERIAL or (SERIAL_SET is not None and self.phase in SERIAL_SET):
            for s, v in list(self.cnt.items()):
                if v > 0 and s != inc and s != OWN.get(eng):
                    self.streams[eng].append(("w", s, v))
            if inc is None and eng in OWN:
                inc = OWN[eng]
        self.streams[eng].append(("o", fn, inc, by))
        if inc is not None:
            self.cnt[inc] = self.cnt.get(inc, 0) + by
            return self.cnt[inc]
        return None


def build_program():
    nc = bass.Bass("TRN2", target_bir_lowering=False)

    xs = nc.dram_tensor("xs", [HR, C], F32, kind="ExternalInput").ap()
    wq = nc.dram_tensor("wq", [C, C], F32, kind="ExternalInput").ap()
    wk = nc.dram_tensor("wk", [C, 512], F32, kind="ExternalInput").ap()
    wv = nc.dram_tensor("wv", [C, 512], F32, kind="ExternalInput").ap()
    wo = nc.dram_tensor("wo", [C, C], F32, kind="ExternalInput").ap()
    wgu = nc.dram_tensor("wgu", [C, 2 * INTER], F32, kind="ExternalInput").ap()
    wdn = nc.dram_tensor("wdn", [INTER, C], F32, kind="ExternalInput").ap()
    rqc = nc.dram_tensor("rqc", [128, 512], F32, kind="ExternalInput").ap()
    rqs = nc.dram_tensor("rqs", [128, 512], F32, kind="ExternalInput").ap()
    rkc = nc.dram_tensor("rkc", [128, 576], F32, kind="ExternalInput").ap()
    rks = nc.dram_tensor("rks", [128, 576], F32, kind="ExternalInput").ap()
    masks = nc.dram_tensor("masks", [128, 1536], BF16, kind="ExternalInput").ap()
    ident = nc.dram_tensor("ident", [128, 128], BF16, kind="ExternalInput").ap()
    ones = nc.dram_tensor("ones", [1, 128], F32, kind="ExternalInput").ap()
    sink = nc.dram_tensor("sink", [1, 16], F32, kind="ExternalInput").ap()
    out = nc.dram_tensor("out", [R, C], F32, kind="ExternalOutput").ap()

    p = Prog()

    SEM_NAMES = (["tab", "tab0", "wv", "wv1", "wk", "wk1", "st", "pe", "dve", "act", "gp",
                  "x0", "x1", "x2", "x3", "wq0", "wq1", "wq2", "wq3", "wqa", "wqb"]
                 + ["xo"]
                 + [f"wgu{i}" for i in range(3)]
                 + [f"wdn{i}" for i in range(5)]
                 + [f"wo{i}" for i in range(4)])
    with contextlib.ExitStack() as es:
        arena_t = es.enter_context(nc.sbuf_tensor("arena", [128, ARENA], U8))
        pall_t = es.enter_context(nc.psum_tensor("pall", [128, 4096], F32))
        sems = {n: es.enter_context(nc.semaphore("s_" + n)) for n in SEM_NAMES}
        block = es.enter_context(nc.Block())
        ar = arena_t[:]
        pall = pall_t[:]

        def cv(off, nel, dt):
            return ar[:, off:off + nel * DSZ[dt]].bitcast(dt)

        xnT = cv(O_XNT, 18432, BF16)
        xnT3 = xnT.rearrange("q (h r) -> q h r", h=16)
        out_sb = cv(O_R1, 16384, F32)
        masks_sb = cv(O_MASKS, 1536, BF16)
        ident_sb = cv(O_IDENT, 128, BF16)
        sinkb = cv(O_SINKB, 16, F32)
        small = cv(O_SMALL, 64, F32)
        qT = cv(O_R3, 16384, BF16)
        kTb = cv(O_R4, 4608, BF16)
        v_sb = cv(O_R4 + 9216, 4608, BF16)
        ones_sb = ar[0:1, O_ONES:O_ONES + 512].bitcast(F32)
        sink_sb = ar[0:1, O_SINK:O_SINK + 64].bitcast(F32)

        def sm(col, par):
            return small[:, col + par:col + par + 1]

        def bank_f(b, lo=0, n=512):
            return pall[:, b * 512 + lo: b * 512 + lo + n]

        def bank_b(b, lo=0, n=1024):
            return pall[:, b * 512:(b + 1) * 512].bitcast(BF16)[:, lo:lo + n]

        tab_cnt = 0
        p.op("sync", lambda e: e.dma_start(out=cv(O_IDENT, 128, BF16), in_=ident), inc="tab0", by=16)
        p.op("sync", lambda e: e.dma_start(out=ones_sb, in_=ones), inc="tab0", by=16)
        tab_first = p.op("sync", lambda e: e.dma_start(out=sink_sb, in_=sink), inc="tab0", by=16)
        for dst_off, nel, dt, src in (
            (O_MASKS, 1536, BF16, masks),
            (O_TABQ, 512, F32, rqc), (O_TABQ + 2048, 512, F32, rqs),
            (O_TABK, 576, F32, rkc), (O_TABK + 2304, 576, F32, rks),
        ):
            d = cv(dst_off, nel, dt)
            tab_cnt = p.op("sync", lambda e, d=d, s=src: e.dma_start(out=d, in_=s), inc="tab", by=16)

        p.op("vector", lambda e: e.memset(sm(SM_EPS, 0), 1.0e-5), inc="dve")

        def x2c(slot):
            return cv(O_R1 + slot * 8192, 2048, F32)

        def emit_xdma(rt, gate_dve):
            d = x2c(rt % 4)
            p.w("sync", "dve", gate_dve)
            return p.op("sync", lambda e, d=d, rt=rt: e.dma_start(
                out=d, in_=xs[rt * 128:(rt + 1) * 128, :]),
                inc="x%d" % (rt % 4), by=16)

        for rt0 in range(4):
            emit_xdma(rt0, 0)

        p.w("tensor", "tab0", tab_first)
        c_sinkmm = p.op("tensor", lambda e: e.matmul(
            bank_f(0, 0, 16), ones_sb, sink_sb, start=True, stop=True), inc="pe")
        p.w("vector", "pe", c_sinkmm)
        c_sinkcp = p.op("vector", lambda e: e.tensor_copy(sinkb, bank_f(0, 0, 16)), inc="dve")

        xn_done = [0] * NTH
        trevac = {}
        trpe = [0] * NTH
        a1s = [0] * NTH
        a3s = [0] * NTH
        a4s = [0] * NTH
        a5s = [0] * NTH
        a6s = [0] * NTH

        def sqb(par2):
            return cv(O_R3 + 20480 + par2 * 8192, 2048, F32)

        def xnc(slot):
            return cv(O_R1 + 32768 + slot * 4096, 2048, BF16)

        def a_sums(t):
            slot, p4 = t % 4, t % 4
            x2 = x2c(slot)
            p.w("vector", "x%d" % slot, 16 * (t // 4 + 1))
            if t >= 4:
                p.w("vector", "act", a3s[t - 4])
            a1s[t] = p.op("vector", lambda e, x2=x2, p4=p4: e.reduce_sum(
                sm(SM_S1, p4), x2, axis=AX.X), inc="dve")
            p.w("scalar", "x%d" % slot, 16 * (t // 4 + 1))
            if t >= 4:
                p.w("scalar", "dve", a4s[t - 4])
            p.op("scalar", lambda e, x2=x2, p4=p4, b2=t % 2: e.activation(
                sqb(b2)[:], x2, AF.Square, accum_out=sm(SM_S2, p4)), inc="act")
            p.w("scalar", "dve", a1s[t])
            p.op("scalar", lambda e, p4=p4: e.activation(
                sm(SM_M, p4), sm(SM_S1, p4), AF.Copy, scale=1.0 / 2048.0), inc="act")
            a3s[t] = p.op("scalar", lambda e, p4=p4: e.activation(
                sm(SM_MSQ, p4), sm(SM_S1, p4), AF.Square, scale=1.0 / 2048.0), inc="act")

        def a_var(t):
            p4 = t % 4
            p.w("vector", "act", a3s[t])
            if t >= 4:
                p.w("vector", "act", a5s[t - 4])
            a4s[t] = p.op("vector", lambda e, p4=p4: e.tensor_scalar(
                sm(SM_VAR, p4), sm(SM_S2, p4), 1.0 / 2048.0, sm(SM_MSQ, p4),
                ALU.mult, ALU.subtract), inc="dve")
            p.w("scalar", "dve", a4s[t])
            a5s[t] = p.op("scalar", lambda e, p4=p4: e.activation(
                sm(SM_LNV, p4), sm(SM_VAR, p4), AF.Ln, bias=sm(SM_EPS, 0)), inc="act")
            p.w("scalar", "act", a5s[t])
            a6s[t] = p.op("scalar", lambda e, p4=p4: e.activation(
                sm(SM_RSTD, p4), sm(SM_LNV, p4), AF.Exp, scale=-0.5), inc="act")

        def a_xn(t):
            slot, p4 = t % 4, t % 4
            p.w("vector", "act", a6s[t])
            if t >= 4:
                p.w("vector", "pe", trpe[t - 4])
            xn_done[t] = p.op("vector", lambda e, x2=x2c(slot), slot=slot, p4=p4: e.tensor_scalar(
                xnc(slot)[:], x2, sm(SM_M, p4), sm(SM_RSTD, p4),
                ALU.subtract, ALU.mult), inc="dve")

        def a_trs(t):
            slot = t % 4
            p.set_phase("A_tr")
            for g in range(4):
                gi = t * 4 + g
                p.w("tensor", "dve", xn_done[t])
                if gi == 0:
                    p.w("tensor", "dve", c_sinkcp)
                if gi >= 2:
                    ev_eng, ev_cnt = trevac[gi - 2]
                    p.w("tensor", ev_eng, ev_cnt)
                base_bank = 4 * (g % 2)
                last = None
                for j in range(4):
                    ht = 4 * g + j
                    last = p.op("tensor", lambda e, slot=slot, ht=ht, b=base_bank + j: e.transpose(
                        bank_b(b, 0, 128), xnc(slot)[:, ht * 128:(ht + 1) * 128], ident_sb),
                        inc="pe" if j == 3 else None)
                trpe[t] = last
                psrc = pall.bitcast(BF16).rearrange("q (b j) -> q b j", b=8)[
                    :, base_bank:base_bank + 4, 0:128]
                pdst = xnT3[:, 4 * g:4 * g + 4, t * 128:(t + 1) * 128]
                if g != 3:
                    p.w("scalar", "pe", last)
                    trevac[gi] = ("act", p.op("scalar", lambda e, pd=pdst, ps=psrc: e.copy(
                        pd, ps), inc="act"))
                else:
                    p.w("vector", "pe", last)
                    trevac[gi] = ("dve", p.op("vector", lambda e, pd=pdst, ps=psrc: e.tensor_copy(
                        pd, ps), inc="dve"))
            p.set_phase("A")

        for t in range(NTH + 3):
            if t >= 4 and t < NTH + 4 - 4:
                emit_xdma(t, xn_done[t - 4])
            if t < NTH:
                a_sums(t)
            if 2 <= t < NTH + 2:
                a_var(t - 2)
            if 3 <= t < NTH + 3:
                a_xn(t - 3)
                a_trs(t - 3)

        trevac_act_last = trevac[34][1]
        trevac_dve_last = trevac[35][1]

        p.w("sync", "dve", xn_done[NTH - 1])
        p.w("sync", "pe", trpe[NTH - 1])
        xo_cnt = 0
        for m in range(NTR):
            xo_cnt = p.op("sync", lambda e, m=m: e.dma_start(
                out=out_sb[:, m * 2048:(m + 1) * 2048],
                in_=xs[64 + m * 128: 64 + (m + 1) * 128, :]), inc="xo", by=16)

        p.set_phase("E")
        def wgu_slot(s):
            return cv(O_R2 + s * 8192, 4096, BF16)

        def wdn_slot(s):
            return cv(O_R3 + s * 4096, 2048, BF16)

        def hT_tile(t):
            return cv(O_R3 + 20480 + t * 2048, 1024, BF16)

        def sg_c(par):
            return cv(O_R4 + par * 4096, 1024, F32)

        gup_gate_done = [0] * NI
        gup_done = [0] * NI
        silu_done = [0] * NI
        hmul_done = [0] * NI
        dn_blk = {}
        dn_evac = {}
        dn_grp_done = [0] * (NI // G)

        wgu_cnt = [0, 0, 0]
        wgu_wait = {}

        def wgu_slot_of(i):
            return (i + 2) % 3

        def emit_wgu_dma(i, gate_pe):
            s = wgu_slot_of(i)
            d = wgu_slot(s).rearrange("q (h j) -> q h j", h=16)
            srcap = wgu[:, i * 256:(i + 1) * 256].rearrange("(h q) j -> q h j", q=128)
            p.w("gpsimd", "pe", gate_pe)
            wgu_cnt[s] += 1
            wgu_wait[i] = (s, 16 * wgu_cnt[s])
            return p.op("gpsimd", lambda e, d=d, s2=srcap: e.dma_start(out=d, in_=s2),
                        inc="wgu%d" % s, by=16)

        def emit_wdn_dma(i, gate_pe):
            d = wdn_slot(i % 5)
            src = wdn[i * 128:(i + 1) * 128, :]
            p.w("gpsimd", "pe", gate_pe)
            return p.op("gpsimd", lambda e, d=d, s=src: e.dma_start(out=d, in_=s),
                        inc="wdn%d" % (i % 5), by=16)

        emit_wgu_dma(0, 0)
        p.w("gpsimd", "dve", xn_done[0])
        emit_wgu_dma(1, 0)
        p.w("gpsimd", "dve", xn_done[1])
        emit_wgu_dma(2, 0)
        p.w("gpsimd", "dve", xn_done[2])
        emit_wdn_dma(0, 0)
        p.w("gpsimd", "dve", xn_done[3])
        emit_wdn_dma(1, 0)
        p.w("gpsimd", "dve", xn_done[4])
        for i in range(2, 5):
            emit_wdn_dma(i, 0)

        def emit_dn_block(gsrc, bl):
            bi = gsrc * 16 + bl
            msub, half = bl // 2, bl % 2
            pairbase = 4 if (bl % 2 == 0) else 6
            if bl == 0:
                for it in range(4 * gsrc, 4 * gsrc + 4):
                    p.w("tensor", "wdn%d" % (it % 5), 16 * (it // 5 + 1))
            if bl == 0:
                p.w("tensor", "dve", hmul_done[4 * gsrc + 3])
            if gsrc == 0 and bl < 2:
                p.w("tensor", "dve", trevac_dve_last)
            if bi >= 2:
                p.w("tensor", "dve", dn_evac[bi - 2])
            if bi == 0:
                p.w("vector", "xo", xo_cnt)
            lastmm = None
            for ii in range(4):
                it = 4 * gsrc + ii
                lhsT = hT_tile((gsrc % 2) * 4 + ii)[:, msub * 128:(msub + 1) * 128]
                rhs = wdn_slot(it % 5)
                for ch in range(2):
                    lastmm = p.op("tensor", lambda e, l=lhsT, r=rhs, half=half, ch=ch,
                                  pb=pairbase, ii=ii: e.matmul(
                        bank_f(pb + ch), l, r[:, half * 1024 + ch * 512: half * 1024 + (ch + 1) * 512],
                        start=(ii == 0), stop=(ii == 3)),
                        inc="pe" if (ii == 3 and ch == 1) else None)
            dn_blk[bi] = lastmm
            p.w("vector", "pe", lastmm)
            psrc = bank_f(pairbase, 0, 1024)
            dsl = out_sb[:, msub * 2048 + half * 1024: msub * 2048 + (half + 1) * 1024]
            dn_evac[bi] = p.op("vector", lambda e, d=dsl, s=psrc: e.tensor_tensor(
                d, d, s, ALU.add), inc="dve")

        for i in range(NI):
            g = i // 4
            p.w("tensor", "wgu%d" % wgu_wait[i][0], wgu_wait[i][1])
            if i == 0:
                p.w("tensor", "act", trevac_act_last)
                p.w("tensor", "dve", trevac_dve_last)
            if i > 0:
                p.w("tensor", "act", silu_done[i - 1])
            slot = wgu_slot(wgu_slot_of(i))
            for ht in range(KT):
                g_st = slot[:, ht * 256: ht * 256 + 128]
                mv0 = xnT[:, ht * 1152 + 64: ht * 1152 + 64 + 512]
                mv1 = xnT[:, ht * 1152 + 64 + 512: ht * 1152 + 64 + 1024]
                p.op("tensor", lambda e, s=g_st, m=mv0, ht=ht: e.matmul(
                    bank_f(0), s, m, start=(ht == 0), stop=(ht == 15)), inc=None)
                gup_gate_done[i] = p.op("tensor", lambda e, s=g_st, m=mv1, ht=ht: e.matmul(
                    bank_f(1), s, m, start=(ht == 0), stop=(ht == 15)),
                    inc="pe" if ht == 15 else None) or gup_gate_done[i]
            if i > 0:
                p.w("tensor", "dve", hmul_done[i - 1])
            for ht in range(KT):
                u_st = slot[:, ht * 256 + 128: ht * 256 + 256]
                mv0 = xnT[:, ht * 1152 + 64: ht * 1152 + 64 + 512]
                mv1 = xnT[:, ht * 1152 + 64 + 512: ht * 1152 + 64 + 1024]
                p.op("tensor", lambda e, s=u_st, m=mv0, ht=ht: e.matmul(
                    bank_f(2), s, m, start=(ht == 0), stop=(ht == 15)), inc=None)
                gup_done[i] = p.op("tensor", lambda e, s=u_st, m=mv1, ht=ht: e.matmul(
                    bank_f(3), s, m, start=(ht == 0), stop=(ht == 15)),
                    inc="pe" if ht == 15 else None) or gup_done[i]
            if g >= 1 and i % 4 == 0:
                for bl in range(16):
                    emit_dn_block(g - 1, bl)
                dn_grp_done[g - 1] = dn_blk[(g - 1) * 16 + 15]
            if i >= 4 and i % 4 == 0:
                for t in range(i + 1, min(i + 5, NI)):
                    emit_wdn_dma(t, dn_grp_done[i // 4 - 1])
            p.w("scalar", "pe", gup_gate_done[i])
            if i >= 2:
                p.w("scalar", "dve", hmul_done[i - 2])
            silu_done[i] = p.op("scalar", lambda e, par=i % 2: e.activation(
                sg_c(par)[:], bank_f(0, 0, 1024), AF.Silu), inc="act")
            p.w("vector", "act", silu_done[i])
            p.w("vector", "pe", gup_done[i])
            if g >= 2:
                p.w("vector", "pe", dn_blk[(g - 2) * 16 + 15])
            ht_dst = hT_tile((g % 2) * 4 + i % 4)
            hmul_done[i] = p.op("vector", lambda e, d=ht_dst, par=i % 2: e.tensor_tensor(
                d[:], sg_c(par)[:], bank_f(2, 0, 1024), ALU.mult), inc="dve")
            if i + 3 < NI:
                emit_wgu_dma(i + 3, gup_done[i])
        for bl in range(16):
            emit_dn_block(NI // 4 - 1, bl)
        dn_grp_done[NI // 4 - 1] = dn_blk[(NI // 4 - 1) * 16 + 15]
        ffn_dn_end_dve = dn_evac[(NI // 4 - 1) * 16 + 15]

        p.set_phase("B")
        ffn_pe_end = dn_grp_done[NI // 4 - 1]
        wvk = cv(O_R2, 8192, BF16)
        wvk3 = wvk.rearrange("q (h j) -> q h j", h=16)
        wkA = cv(O_WQ1, 4096, BF16)
        wkB = cv(O_R4 + 9216, 4096, BF16)
        p.w("gpsimd", "pe", gup_done[16])
        p.op("gpsimd", lambda e: e.dma_start(
            out=wkA.rearrange("q (h j) -> q h j", h=8),
            in_=wk[0:1024, :].rearrange("(h q) j -> q h j", q=128)), inc="wk", by=16)
        p.w("gpsimd", "pe", gup_done[17])
        p.op("gpsimd", lambda e: e.dma_start(
            out=wkB.rearrange("q (h j) -> q h j", h=8),
            in_=wk[1024:2048, :].rearrange("(h q) j -> q h j", q=128)), inc="wk1", by=16)
        p.w("gpsimd", "pe", gup_done[61])
        p.op("gpsimd", lambda e: e.dma_start(
            out=wvk3[:, 0:8, :],
            in_=wv[0:1024, :].rearrange("(h q) j -> q h j", q=128)), inc="wv", by=16)
        p.w("gpsimd", "pe", gup_done[62])
        p.op("gpsimd", lambda e: e.dma_start(
            out=wvk3[:, 8:16, :],
            in_=wv[1024:2048, :].rearrange("(h q) j -> q h j", q=128)), inc="wv1", by=16)

        rkc_sb = cv(O_TABK, 576, F32)
        rks_sb = cv(O_TABK + 2304, 576, F32)
        scr_a = cv(O_SCR, 256, F32)
        scr_b = cv(O_SCR + 1024, 256, F32)
        scr_c = cv(O_SCR2, 256, F32)
        scr_d = cv(O_SCR2 + 1024, 256, F32)

        def natc(par):
            return cv(O_NAT + par * 1024, 512, BF16)

        kmm = [0] * NTH
        krope = [0] * NTH
        ktr2 = {}
        ktrevac = {}

        def k_mms(rt):
            par = rt % 2
            p.w("tensor", "wk", 16)
            p.w("tensor", "wk1", 16)
            if rt >= 2:
                p.w("tensor", "dve", krope[rt - 2])
            else:
                p.w("tensor", "dve", ffn_dn_end_dve)
                p.w("tensor", "act", silu_done[NI - 1])
            for ht in range(KT):
                wsl = wkA[:, ht * 512:(ht + 1) * 512] if ht < 8 else \
                    wkB[:, (ht - 8) * 512:(ht - 7) * 512]
                kmm[rt] = p.op("tensor", lambda e, ht=ht, rt=rt, par=par, w=wsl: e.matmul(
                    bank_f(par), xnT[:, ht * 1152 + rt * 128: ht * 1152 + (rt + 1) * 128],
                    w, start=(ht == 0), stop=(ht == 15)),
                    inc="pe" if ht == 15 else None) or kmm[rt]

        def k_rope(rt):
            par = rt % 2
            if rt == 0:
                p.w("vector", "tab", tab_cnt)
            p.w("vector", "pe", kmm[rt])
            if rt >= 2:
                p.w("vector", "pe", ktr2[(rt - 2) * 4 + 3])
            ps3 = bank_f(par).rearrange("q (h d) -> q h d", h=4)
            r3, i3 = ps3[:, :, 0:64], ps3[:, :, 64:128]
            cos3 = rkc_sb[:, rt * 64:(rt + 1) * 64][:, None, :].broadcast_to((128, 4, 64))
            sin3 = rks_sb[:, rt * 64:(rt + 1) * 64][:, None, :].broadcast_to((128, 4, 64))
            sa3 = scr_a.rearrange("q (h d) -> q h d", h=4)
            sb3 = scr_b.rearrange("q (h d) -> q h d", h=4)
            sc3 = scr_c.rearrange("q (h d) -> q h d", h=4)
            sd3 = scr_d.rearrange("q (h d) -> q h d", h=4)
            n3 = natc(par).rearrange("q (h d) -> q h d", h=4)
            p.fence("vector")
            p.op("vector", lambda e, a=sa3, r=r3, c=cos3: e.tensor_tensor(a, r, c, ALU.mult), inc="dve")
            p.op("vector", lambda e, b=sb3, i=i3, s=sin3: e.tensor_tensor(b, i, s, ALU.mult), inc="dve")
            p.op("vector", lambda e, a=sc3, r=r3, s=sin3: e.tensor_tensor(a, r, s, ALU.mult), inc="dve")
            p.op("vector", lambda e, b=sd3, i=i3, c=cos3: e.tensor_tensor(b, i, c, ALU.mult), inc="dve")
            p.fence("vector")
            p.op("vector", lambda e, o=n3[:, :, 0:64], a=sa3, b=sb3: e.tensor_tensor(
                o, a, b, ALU.subtract), inc="dve")
            krope[rt] = p.op("vector", lambda e, o=n3[:, :, 64:128], a=sc3, b=sd3: e.tensor_tensor(
                o, a, b, ALU.add), inc="dve")

        def k_trs(rt):
            par = rt % 2
            p.w("tensor", "dve", krope[rt])
            for kv in range(4):
                kti = rt * 4 + kv
                if kti >= 2:
                    p.w("tensor", "act", ktrevac[kti - 2])
                else:
                    p.w("tensor", "dve", ffn_dn_end_dve)
                b = 4 + (kv % 2)
                ktr2[kti] = p.op("tensor", lambda e, par=par, kv=kv, b=b: e.transpose(
                    bank_b(b, 0, 128), natc(par)[:, kv * 128:(kv + 1) * 128], ident_sb), inc="pe")
                p.w("scalar", "pe", ktr2[kti])
                ktrevac[kti] = p.op("scalar", lambda e, kv=kv, rt=rt, b=b: e.copy(
                    kTb[:, kv * 1152 + rt * 128: kv * 1152 + (rt + 1) * 128],
                    bank_b(b, 0, 128)), inc="act")

        for rt in range(NTH + 2):
            if rt < NTH:
                k_mms(rt)
            if rt >= 2:
                k_trs(rt - 2)
            if rt < NTH:
                k_rope(rt)

        vmm = [0] * NTH
        vevac = [0] * NTH
        for rt in range(NTH):
            par = rt % 2
            p.w("tensor", "wv", 16)
            p.w("tensor", "wv1", 16)
            if rt >= 2:
                p.w("tensor", "dve", vevac[rt - 2])
            else:
                p.w("tensor", "dve", krope[8 - rt])
            for ht in range(KT):
                vmm[rt] = p.op("tensor", lambda e, ht=ht, rt=rt, par=par: e.matmul(
                    bank_f(par), xnT[:, ht * 1152 + rt * 128: ht * 1152 + (rt + 1) * 128],
                    wvk[:, ht * 512:(ht + 1) * 512],
                    start=(ht == 0), stop=(ht == 15)),
                    inc="pe" if ht == 15 else None) or vmm[rt]
            p.w("vector", "pe", vmm[rt])
            vevac[rt] = p.op("vector", lambda e, rt=rt, par=par: e.tensor_copy(
                v_sb[:, rt * 512:(rt + 1) * 512], bank_f(par)), inc="dve")

        rqc_sb = cv(O_TABQ, 512, F32)
        rqs_sb = cv(O_TABQ + 2048, 512, F32)

        WQ_STAGE = {
            (1, 0): (O_R2 + 16384, "wq2", 16),
            (1, 1): (O_WQ1, "wq3", 16),
            (0, 0): (O_R3 + 24576, "wqa", 16),
            (0, 1): (O_R3 + 16384, "wqb", 16),
            (2, 0): (O_R2, "wq0", 16),
            (2, 1): (O_R2 + 8192, "wq1", 16),
            (3, 0): (O_R2 + 16384, "wq2", 32),
            (3, 1): (O_WQ1, "wq3", 32),
        }

        def wq_home(dc4, half):
            return cv(WQ_STAGE[(dc4, half)][0], 4096, BF16)

        qmm = {}
        qrope = {}
        qtr2 = {}
        qtrevac = {}
        qT_ready = [0] * NH

        def emit_wq_dma(dc4, half, gate_pe):
            off, sem, _ = WQ_STAGE[(dc4, half)]
            d = wq_home(dc4, half).rearrange("q (h j) -> q h j", h=8)
            srcap = wq[half * 1024:(half + 1) * 1024,
                       dc4 * 512:(dc4 + 1) * 512].rearrange("(h q) j -> q h j", q=128)
            p.w("gpsimd", "pe", gate_pe)
            return p.op("gpsimd", lambda e, d=d, s2=srcap: e.dma_start(out=d, in_=s2),
                        inc=sem, by=16)

        emit_wq_dma(1, 0, gup_done[NI - 1])
        emit_wq_dma(0, 0, ffn_pe_end)
        emit_wq_dma(0, 1, ffn_pe_end)
        emit_wq_dma(1, 1, kmm[NTH - 1])
        emit_wq_dma(2, 0, vmm[NTH - 1])
        emit_wq_dma(2, 1, vmm[NTH - 1])
        DC4ORDER = [1, 0, 2, 3]

        def q_mms(idx):
            dc4, mt = DC4ORDER[idx // 8], idx % 8
            par = idx % 4
            for half in range(2):
                _, sem, cnt = WQ_STAGE[(dc4, half)]
                p.w("tensor", sem, cnt)
            if idx >= 4:
                p.w("tensor", "dve", qrope[idx - 4])
            else:
                p.w("tensor", "dve", vevac[NTH - 1])
            for half in range(2):
                slot = wq_home(dc4, half)
                for ht8 in range(8):
                    ht = half * 8 + ht8
                    qmm[idx] = p.op("tensor", lambda e, ht=ht, ht8=ht8, mt=mt, par=par, s=slot: e.matmul(
                        bank_f(par),
                        xnT[:, ht * 1152 + 64 + mt * 128: ht * 1152 + 64 + (mt + 1) * 128],
                        s[:, ht8 * 512:(ht8 + 1) * 512],
                        start=(ht == 0), stop=(ht == 15)),
                        inc="pe" if ht == 15 else None) or qmm.get(idx, 0)

        def q_rope(idx):
            mt = idx % 8
            par = idx % 4
            p.w("vector", "pe", qmm[idx])
            if idx >= 2:
                p.w("vector", "pe", qtr2[(idx - 2) * 4 + 3])
            else:
                p.w("vector", "pe", ktr2[(8 - idx) * 4 + 3])
            ps3 = bank_f(par).rearrange("q (h d) -> q h d", h=4)
            r3, i3 = ps3[:, :, 0:64], ps3[:, :, 64:128]
            cos3 = rqc_sb[:, mt * 64:(mt + 1) * 64][:, None, :].broadcast_to((128, 4, 64))
            sin3 = rqs_sb[:, mt * 64:(mt + 1) * 64][:, None, :].broadcast_to((128, 4, 64))
            sa3 = scr_a.rearrange("q (h d) -> q h d", h=4)
            sb3 = scr_b.rearrange("q (h d) -> q h d", h=4)
            sc3 = scr_c.rearrange("q (h d) -> q h d", h=4)
            sd3 = scr_d.rearrange("q (h d) -> q h d", h=4)
            n3 = natc(idx % 2).rearrange("q (h d) -> q h d", h=4)
            p.fence("vector")
            p.op("vector", lambda e, a=sa3, r=r3, c=cos3: e.tensor_tensor(a, r, c, ALU.mult), inc="dve")
            p.op("vector", lambda e, b=sb3, i=i3, s=sin3: e.tensor_tensor(b, i, s, ALU.mult), inc="dve")
            p.op("vector", lambda e, a=sc3, r=r3, s=sin3: e.tensor_tensor(a, r, s, ALU.mult), inc="dve")
            p.op("vector", lambda e, b=sd3, i=i3, c=cos3: e.tensor_tensor(b, i, c, ALU.mult), inc="dve")
            p.fence("vector")
            p.op("vector", lambda e, o=n3[:, :, 0:64], a=sa3, b=sb3: e.tensor_tensor(
                o, a, b, ALU.subtract), inc="dve")
            qrope[idx] = p.op("vector", lambda e, o=n3[:, :, 64:128], a=sc3, b=sd3:
                              e.tensor_tensor(o, a, b, ALU.add), inc="dve")

        def q_trs(idx):
            dc4, mt = DC4ORDER[idx // 8], idx % 8
            p.w("tensor", "dve", qrope[idx])
            for j in range(4):
                tri = idx * 4 + j
                h = dc4 * 4 + j
                if tri >= 2:
                    p.w("tensor", "act", qtrevac[tri - 2])
                else:
                    p.w("tensor", "act", ktrevac[34 + tri])
                b = 4 + (j % 2)
                qtr2[tri] = p.op("tensor", lambda e, par2=idx % 2, j=j, b=b: e.transpose(
                    bank_b(b, 0, 128), natc(par2)[:, j * 128:(j + 1) * 128], ident_sb),
                    inc="pe")
                p.w("scalar", "pe", qtr2[tri])
                qtrevac[tri] = p.op("scalar", lambda e, h=h, mt=mt, b=b: e.copy(
                    qT[:, h * 1024 + mt * 128: h * 1024 + (mt + 1) * 128],
                    bank_b(b, 0, 128)), inc="act")
                qT_ready[h] = qtrevac[tri]

        for idx in range(32 + 2):
            if idx < 32:
                q_mms(idx)
            if idx >= 2:
                q_trs(idx - 2)
            if idx < 32:
                q_rope(idx)
            if idx == 7:
                emit_wq_dma(3, 0, qmm[idx])
                emit_wq_dma(3, 1, qmm[idx])

        q_end_pe = qmm[31]

        def wo_tile(dt):
            if dt <= 8:
                off = O_XNT + dt * 4096
            elif dt <= 14:
                off = O_R2 + (dt - 9) * 4096
            else:
                off = O_R3 + 32768
            return cv(off, 2048, BF16)

        for dt in [15, 9, 10, 11, 12] + list(range(9)) + [13, 14]:
            d = wo_tile(dt)
            src = wo[dt * 128:(dt + 1) * 128, :]
            eng = "gpsimd"
            if dt == 15:
                gate = ffn_pe_end
            elif 9 <= dt <= 12:
                gate = qmm[23]
            else:
                gate = q_end_pe
            p.w(eng, "pe", gate)
            p.op(eng, lambda e, d=d, s=src: e.dma_start(out=d, in_=s),
                 inc="wo%d" % (dt // 4), by=16)

        p.set_phase("C")
        def e2_c(p4):
            return cv(O_ESB + p4 * 2048, 512, F32)

        def p2_c(p4):
            return cv(O_PSB + p4 * 1024, 512, BF16)

        def pT2_c(p4):
            return cv(O_PTSB + p4 * 1024, 512, BF16)

        qTh3 = qT.rearrange("q (h c) -> q h c", h=16)

        c_s2 = {}
        c_e2 = {}
        c_p2 = {}
        c_tr2 = {}
        c_t2 = {}
        c_os2 = {}
        oevac2 = {}
        woevac_a = [0] * NTR
        woevac_b = [0] * NTR

        PSTEPS = [(qt, hp) for qt in range(NTR) for hp in range(NH // 2)]
        NP = len(PSTEPS)

        def c_smm(s):
            qt, hp = PSTEPS[s]
            h0 = 2 * hp
            kind = 0 if qt == 0 else (2 if qt == 7 else 1)
            sb = s % 2
            kvh = hp // 2
            if s >= 2:
                p.w("tensor", "act", c_e2[s - 2])
            if s == 0:
                p.w("tensor", "dve", qrope[31])
                p.w("tensor", "tab", tab_cnt)
            if qt == 0:
                p.w("tensor", "act", qT_ready[h0 + 1])
                p.w("tensor", "act", qT_ready[h0])
            for hh in range(2):
                p.op("tensor", lambda e, h=h0 + hh, qt=qt, sb=sb, kvh=kvh, hh=hh: e.matmul(
                    bank_f(sb, hh * 256, 256),
                    qT[:, h * 1024 + qt * 128: h * 1024 + (qt + 1) * 128],
                    kTb[:, kvh * 1152 + qt * 128: kvh * 1152 + qt * 128 + 256],
                    start=(hh == 0), stop=False))
            for hh in range(2):
                c_s2[s] = p.op("tensor", lambda e, sb=sb, kind=kind, hh=hh: e.matmul(
                    bank_f(sb, hh * 256, 256), ident_sb,
                    masks_sb[:, kind * 512 + hh * 256: kind * 512 + (hh + 1) * 256],
                    start=False, stop=(hh == 1)), inc="pe" if hh == 1 else None) or c_s2.get(s)

        def c_exp(s):
            qt, hp = PSTEPS[s]
            h0 = 2 * hp
            sb = s % 2
            p4 = s % 4
            p.w("scalar", "pe", c_s2[s])
            if s >= 4:
                p.w("scalar", "dve", c_p2[s - 4])
            for hh in range(2):
                c_e2[s] = p.op("scalar", lambda e, sb=sb, p4=p4, hh=hh, h=h0 + hh: e.activation(
                    e2_c(p4)[:, hh * 256:(hh + 1) * 256],
                    bank_f(sb, hh * 256, 256), AF.Exp,
                    bias=sinkb[:, h:h + 1],
                    accum_out=sm(SM_DEN, 2 * p4 + hh)), inc="act")

        def c_recip_pmult(s):
            p4 = s % 4
            p.w("vector", "act", c_e2[s])
            p.op("vector", lambda e, p4=p4: e.reciprocal(
                small[:, SM_RC + 2 * p4: SM_RC + 2 * p4 + 2],
                small[:, SM_DEN + 2 * p4: SM_DEN + 2 * p4 + 2]), inc="dve")
            p.fence("vector")
            if s >= 4:
                p.w("vector", "pe", c_tr2[s - 4])
            for hh in range(2):
                c_p2[s] = p.op("vector", lambda e, p4=p4, hh=hh: e.tensor_scalar(
                    p2_c(p4)[:, hh * 256:(hh + 1) * 256],
                    e2_c(p4)[:, hh * 256:(hh + 1) * 256],
                    sm(SM_RC, 2 * p4 + hh), None, ALU.mult), inc="dve")

        def c_tr_st(s):
            p4 = s % 4
            tb = 6 + s % 2
            p.w("tensor", "dve", c_p2[s])
            if s >= 2:
                p.w("tensor", "act", c_t2[s - 2])
            last = None
            for tj in range(4):
                jj, hh = tj // 2, tj % 2
                src = p2_c(p4)[:, hh * 256 + jj * 128: hh * 256 + (jj + 1) * 128]
                last = p.op("tensor", lambda e, tb=tb, tj=tj, src=src: e.transpose(
                    bank_b(tb, tj * 128, 128), src, ident_sb),
                    inc="pe" if tj == 3 else None) or last
            c_tr2[s] = last

        def c_ptevac(s):
            p4 = s % 4
            tb = 6 + s % 2
            p.w("scalar", "pe", c_tr2[s])
            if s >= 4:
                p.w("scalar", "pe", c_os2[s - 4])
            c_t2[s] = p.op("scalar", lambda e, p4=p4, tb=tb: e.copy(
                pT2_c(p4)[:], bank_b(tb, 0, 512)), inc="act")

        def c_o(s):
            qt, hp = PSTEPS[s]
            p4 = s % 4
            ob = 2 + s % 2
            kvh = hp // 2
            p.w("tensor", "act", c_t2[s])
            if s >= 2:
                p.w("tensor", "act", oevac2[s - 2])
            co = None
            for j in range(2):
                lhsT = v_sb[:, (qt + j) * 512 + kvh * 128: (qt + j) * 512 + (kvh + 1) * 128]
                co = p.op("tensor", lambda e, l=lhsT, ob=ob, p4=p4, j=j: e.matmul(
                    bank_f(ob, 0, 256), l, pT2_c(p4)[:, j * 256:(j + 1) * 256],
                    start=(j == 0), stop=(j == 1)), inc="pe" if j == 1 else None) or co
            c_os2[s] = co

        def c_oevac(s):
            qt, hp = PSTEPS[s]
            ob = 2 + s % 2
            h0 = 2 * hp
            p.w("scalar", "pe", c_os2[s])
            psrc = bank_f(ob, 0, 256).rearrange("q (h c) -> q h c", h=2)
            pdst = qTh3[:, h0:h0 + 2, qt * 128:(qt + 1) * 128]
            oevac2[s] = p.op("scalar", lambda e, pd=pdst, ps=psrc: e.copy(pd, ps),
                             inc="act")

        def c_wo_unit(m, u):
            pass_, dt = u // 16, u % 16
            p.w("tensor", "wo%d" % (dt // 4), 64)
            p.w("tensor", "act", oevac2[m * 8 + dt // 2])
            if u == 0:
                if m == 0:
                    p.w("tensor", "act", qtrevac[127])
                else:
                    p.w("tensor", "dve", woevac_b[m - 1])
            if u == 16:
                p.w("tensor", "dve", woevac_a[m])
            last = None
            for chh in range(2):
                col = (pass_ * 2 + chh) * 512
                last = p.op("tensor", lambda e, dt=dt, m=m, chh=chh, col=col: e.matmul(
                    bank_f(4 + chh),
                    qT[:, dt * 1024 + m * 128: dt * 1024 + (m + 1) * 128],
                    wo_tile(dt)[:, col:col + 512],
                    start=(dt == 0), stop=(dt == 15)),
                    inc="pe" if chh == 1 else None) or last
            if u == 15:
                p.w("vector", "pe", last)
                woevac_a[m] = p.op("vector", lambda e, m=m: e.tensor_tensor(
                    out_sb[:, m * 2048: m * 2048 + 1024],
                    out_sb[:, m * 2048: m * 2048 + 1024],
                    bank_f(4, 0, 1024), ALU.add), inc="dve")
            if u == 31:
                p.w("vector", "pe", last)
                woevac_b[m] = p.op("vector", lambda e, m=m: e.tensor_tensor(
                    out_sb[:, m * 2048 + 1024: (m + 1) * 2048],
                    out_sb[:, m * 2048 + 1024: (m + 1) * 2048],
                    bank_f(4, 0, 1024), ALU.add), inc="dve")

        for slot in range(NP + 10):
            if 4 <= slot < NP + 4:
                c_ptevac(slot - 4)
            if 6 <= slot < NP + 6:
                c_oevac(slot - 6)
            if slot < NP:
                c_smm(slot)
            if 5 <= slot < NP + 5:
                c_o(slot - 5)
            if 3 <= slot < NP + 3:
                c_tr_st(slot - 3)
            if slot >= 10:
                m_u, jj = (slot - 2) // 8 - 1, (slot - 2) % 8
                if m_u < NTR:
                    for du in range(4):
                        c_wo_unit(m_u, 4 * jj + du)
            if 1 <= slot < NP + 1:
                c_exp(slot - 1)
            if 2 <= slot < NP + 2:
                c_recip_pmult(slot - 2)

        for mtile in range(NTR):
            p.w("sync", "dve", woevac_b[mtile])
            p.op("sync", lambda e, m=mtile: e.dma_start(
                out=out[m * 128:(m + 1) * 128, :],
                in_=out_sb[:, m * 2048:(m + 1) * 2048]), inc="st", by=16)
        p.w("sync", "st", 16 * NTR)


        def replay(eng, items):
            for it in items:
                if it[0] == "w":
                    eng.wait_ge(sems[it[1]], it[2])
                else:
                    inst = it[1](eng)
                    if it[2] is not None:
                        inst.then_inc(sems[it[2]], it[3])

        @block.tensor
        def _(t):
            replay(t, p.streams["tensor"])

        @block.vector
        def _(v):
            replay(v, p.streams["vector"])

        @block.scalar
        def _(s):
            replay(s, p.streams["scalar"])

        @block.sync
        def _(s):
            replay(s, p.streams["sync"])

        @block.gpsimd
        def _(g):
            replay(g, p.streams["gpsimd"])

    return nc


def _host_inputs(L0_attn_x, L0_attn_wq, L0_attn_wk, L0_attn_wv, L0_attn_wo,
                 L0_attn_sink, L0_ffn_w_gate_up, L0_ffn_w_down):
    xf = np.asarray(L0_attn_x, dtype=np.float32).reshape(T, C)
    wq = np.asarray(L0_attn_wq, dtype=np.float32)
    wk = np.asarray(L0_attn_wk, dtype=np.float32)
    wv = np.ascontiguousarray(np.asarray(L0_attn_wv, dtype=np.float32))
    wo = np.ascontiguousarray(np.asarray(L0_attn_wo, dtype=np.float32))
    sink = np.asarray(L0_attn_sink, dtype=np.float32).reshape(1, NH)
    wgu_in = np.asarray(L0_ffn_w_gate_up, dtype=np.float32)
    wdn_in = np.asarray(L0_ffn_w_down, dtype=np.float32)

    perm = np.concatenate([np.arange(0, HD, 2), np.arange(1, HD, 2)])
    idx_q = np.concatenate([h * HD + perm for h in range(NH)])
    idx_k = np.concatenate([h * HD + perm for h in range(NKV)])
    wq_p = np.ascontiguousarray(wq[:, idx_q])
    wk_p = np.ascontiguousarray(wk[:, idx_k])

    wguT = wgu_in.T
    wgu2 = np.ascontiguousarray(
        wguT.reshape(C, 2, NI, 128).transpose(0, 2, 1, 3).reshape(C, 2 * INTER))
    wdnT = np.ascontiguousarray(wdn_in.T)

    ident = np.eye(128, dtype=ml_dtypes.bfloat16)
    ones = np.ones((1, 128), dtype=np.float32)

    inv = (1.0 / (10000.0 ** (np.arange(64, dtype=np.float64) / 64.0)))

    base_mask = np.full((128, 256), NEG, dtype=np.float32)
    rr = np.arange(128)[:, None]
    jj = np.arange(256)[None, :]
    band = (jj >= rr) & (jj <= rr + 128)
    base_mask[band] = 0.0

    in_maps = []
    for c in range(NCORES):
        start = c * R
        lo, hi = start - HALO, start + R + HALO
        slab = np.zeros((HR, C), dtype=np.float32)
        s0, s1 = max(lo, 0), min(hi, T)
        slab[s0 - lo: s1 - lo] = xf[s0:s1]

        posq = (start + np.arange(R, dtype=np.float64))
        angq = posq[:, None] * inv[None, :]
        rqc = (np.cos(angq) * SCALE).astype(np.float32).reshape(NTR, 128, 64)
        rqs = (np.sin(angq) * SCALE).astype(np.float32).reshape(NTR, 128, 64)
        rqc = np.ascontiguousarray(rqc.transpose(1, 0, 2).reshape(128, NTR * 64))
        rqs = np.ascontiguousarray(rqs.transpose(1, 0, 2).reshape(128, NTR * 64))
        posk = (start - HALO + np.arange(HR, dtype=np.float64))
        angk = posk[:, None] * inv[None, :]
        rkc = np.cos(angk).astype(np.float32).reshape(NTH, 128, 64)
        rks = np.sin(angk).astype(np.float32).reshape(NTH, 128, 64)
        rkc = np.ascontiguousarray(rkc.transpose(1, 0, 2).reshape(128, NTH * 64))
        rks = np.ascontiguousarray(rks.transpose(1, 0, 2).reshape(128, NTH * 64))

        m_first = base_mask.copy()
        m_mid = base_mask.copy()
        m_last = base_mask.copy()
        if c == 0:
            m_first[:, :HALO] = NEG
        if c == NCORES - 1:
            m_last[:, 192:] = NEG
        masks = np.ascontiguousarray(np.concatenate(
            [m_first, m_first, m_mid, m_mid, m_last, m_last],
            axis=1)).astype(ml_dtypes.bfloat16)

        in_maps.append({
            "xs": slab, "wq": wq_p, "wk": wk_p, "wv": wv, "wo": wo,
            "wgu": wgu2, "wdn": wdnT, "rqc": rqc, "rqs": rqs,
            "rkc": rkc, "rks": rks, "masks": masks, "ident": ident,
            "ones": ones, "sink": sink,
        })
    return in_maps


def kernel(L0_attn_x, L0_attn_wq, L0_attn_wk, L0_attn_wv, L0_attn_wo,
           L0_attn_sink, L0_ffn_w_gate_up, L0_ffn_w_down):
    in_maps = _host_inputs(L0_attn_x, L0_attn_wq, L0_attn_wk, L0_attn_wv,
                           L0_attn_wo, L0_attn_sink, L0_ffn_w_gate_up,
                           L0_ffn_w_down)
    nc = build_program()
    res = run_bass_kernel_spmd(nc, in_maps, core_ids=list(range(NCORES)))
    outs = [np.asarray(r["out"], dtype=np.float32) for r in res.results]
    return np.concatenate(outs, axis=0).reshape(1, T, C)


if __name__ == "__main__":
    rng = np.random.default_rng(0)
    args = {
        "L0_attn_x": rng.standard_normal((1, T, C), dtype=np.float32),
        "L0_attn_wq": rng.standard_normal((C, C), dtype=np.float32) * 0.02,
        "L0_attn_wk": rng.standard_normal((C, 512), dtype=np.float32) * 0.02,
        "L0_attn_wv": rng.standard_normal((C, 512), dtype=np.float32) * 0.02,
        "L0_attn_wo": rng.standard_normal((C, C), dtype=np.float32) * 0.02,
        "L0_attn_sink": rng.standard_normal((1, NH, 1, 1)).astype(np.float32),
        "L0_ffn_w_gate_up": rng.standard_normal((2 * INTER, C), dtype=np.float32) * 0.02,
        "L0_ffn_w_down": rng.standard_normal((C, INTER), dtype=np.float32) * 0.02,
    }
    y = kernel(**args)
    print(y.shape, y.dtype)


# version 29
# speedup vs baseline: 1.0353x; 1.0353x over previous
import contextlib
import numpy as np
import ml_dtypes
import concourse.bass as bass
import concourse.mybir as mybir
from concourse.bass_utils import run_bass_kernel_spmd

F32 = mybir.dt.float32
BF16 = mybir.dt.bfloat16
U8 = mybir.dt.uint8
AF = mybir.ActivationFunctionType
ALU = mybir.AluOpType
AX = mybir.AxisListType

C = 2048
T = 8192
NCORES = 8
R = 1024
HALO = 64
HR = R + 2 * HALO
NH = 16
NKV = 4
HD = 128
KT = 16
INTER = 8192
NI = 64
G = 4
NTR = 8
NTH = 9
NEG = -1.0e30
SCALE = 1.0 / float(np.sqrt(128.0))

ARENA = 209920
O_XNT = 0
O_R1 = 36864
O_R2 = 102400
O_R3 = 126976
O_R4 = 163840
O_R5 = 182272
O_MASKS = O_R5
O_IDENT = O_R5 + 3072
O_SINKB = O_R5 + 3328
O_SMALL = O_R5 + 3392
O_TABQ = O_R5 + 3648
O_TABK = O_R5 + 7744
O_SCR = O_R5 + 12352
O_NAT = O_R5 + 14400
O_SCR2 = O_R5 + 25216
O_WQ1 = O_R5 + 16448
O_ONES = O_R5 + 24640
O_SINK = O_R5 + 25152
O_ESB = O_TABQ
O_PSB = O_SCR
O_PTSB = O_WQ1

SM_S1, SM_S2, SM_M, SM_MSQ, SM_VAR, SM_LNV, SM_RSTD = 0, 4, 8, 12, 16, 20, 24
SM_EPS = 28
SM_DEN, SM_RC = 40, 56

DSZ = {F32: 4, BF16: 2}
SERIAL = False
SERIAL_SET = set()
OWN = {"tensor": "pe", "vector": "dve", "scalar": "act"}


class Prog:

    def __init__(self):
        self.streams = {k: [] for k in ("tensor", "vector", "scalar", "sync", "gpsimd")}
        self.cnt = {}
        self.phase = "A"

    def set_phase(self, name):
        self.phase = name

    def w(self, eng, sem, val):
        if val > 0:
            self.streams[eng].append(("w", sem, val))

    def fence(self, eng):
        own = OWN[eng]
        self.w(eng, own, self.cnt.get(own, 0))

    def op(self, eng, fn, inc=None, by=1):
        if SERIAL or (SERIAL_SET is not None and self.phase in SERIAL_SET):
            for s, v in list(self.cnt.items()):
                if v > 0 and s != inc and s != OWN.get(eng):
                    self.streams[eng].append(("w", s, v))
            if inc is None and eng in OWN:
                inc = OWN[eng]
        self.streams[eng].append(("o", fn, inc, by))
        if inc is not None:
            self.cnt[inc] = self.cnt.get(inc, 0) + by
            return self.cnt[inc]
        return None


def build_program():
    nc = bass.Bass("TRN2", target_bir_lowering=False)

    xs = nc.dram_tensor("xs", [HR, C], F32, kind="ExternalInput").ap()
    wq = nc.dram_tensor("wq", [C, C], F32, kind="ExternalInput").ap()
    wk = nc.dram_tensor("wk", [C, 512], F32, kind="ExternalInput").ap()
    wv = nc.dram_tensor("wv", [C, 512], F32, kind="ExternalInput").ap()
    wo = nc.dram_tensor("wo", [C, C], F32, kind="ExternalInput").ap()
    wgu = nc.dram_tensor("wgu", [C, 2 * INTER], F32, kind="ExternalInput").ap()
    wdn = nc.dram_tensor("wdn", [INTER, C], F32, kind="ExternalInput").ap()
    rqc = nc.dram_tensor("rqc", [128, 512], F32, kind="ExternalInput").ap()
    rqs = nc.dram_tensor("rqs", [128, 512], F32, kind="ExternalInput").ap()
    rkc = nc.dram_tensor("rkc", [128, 576], F32, kind="ExternalInput").ap()
    rks = nc.dram_tensor("rks", [128, 576], F32, kind="ExternalInput").ap()
    masks = nc.dram_tensor("masks", [128, 1536], BF16, kind="ExternalInput").ap()
    ident = nc.dram_tensor("ident", [128, 128], BF16, kind="ExternalInput").ap()
    ones = nc.dram_tensor("ones", [1, 128], F32, kind="ExternalInput").ap()
    sink = nc.dram_tensor("sink", [1, 16], F32, kind="ExternalInput").ap()
    out = nc.dram_tensor("out", [R, C], F32, kind="ExternalOutput").ap()

    p = Prog()

    SEM_NAMES = (["tab", "tab0", "wv", "wv1", "wk", "wk1", "st", "pe", "dve", "act", "gp",
                  "x0", "x1", "x2", "x3", "x4", "x5",
                  "wq0", "wq1", "wq2", "wq3", "wqa", "wqb"]
                 + ["xo"]
                 + [f"wgu{i}" for i in range(3)]
                 + [f"wdn{i}" for i in range(5)]
                 + [f"wo{i}" for i in range(4)])
    with contextlib.ExitStack() as es:
        arena_t = es.enter_context(nc.sbuf_tensor("arena", [128, ARENA], U8))
        pall_t = es.enter_context(nc.psum_tensor("pall", [128, 4096], F32))
        sems = {n: es.enter_context(nc.semaphore("s_" + n)) for n in SEM_NAMES}
        block = es.enter_context(nc.Block())
        ar = arena_t[:]
        pall = pall_t[:]

        def cv(off, nel, dt):
            return ar[:, off:off + nel * DSZ[dt]].bitcast(dt)

        xnT = cv(O_XNT, 18432, BF16)
        xnT3 = xnT.rearrange("q (h r) -> q h r", h=16)
        out_sb = cv(O_R1, 16384, F32)
        masks_sb = cv(O_MASKS, 1536, BF16)
        ident_sb = cv(O_IDENT, 128, BF16)
        sinkb = cv(O_SINKB, 16, F32)
        small = cv(O_SMALL, 64, F32)
        qT = cv(O_R3, 16384, BF16)
        kTb = cv(O_R4, 4608, BF16)
        v_sb = cv(O_R4 + 9216, 4608, BF16)
        ones_sb = ar[0:1, O_ONES:O_ONES + 512].bitcast(F32)
        sink_sb = ar[0:1, O_SINK:O_SINK + 64].bitcast(F32)

        def sm(col, par):
            return small[:, col + par:col + par + 1]

        def bank_f(b, lo=0, n=512):
            return pall[:, b * 512 + lo: b * 512 + lo + n]

        def bank_b(b, lo=0, n=1024):
            return pall[:, b * 512:(b + 1) * 512].bitcast(BF16)[:, lo:lo + n]

        tab_cnt = 0
        p.op("sync", lambda e: e.dma_start(out=cv(O_IDENT, 128, BF16), in_=ident), inc="tab0", by=16)
        p.op("sync", lambda e: e.dma_start(out=ones_sb, in_=ones), inc="tab0", by=16)
        tab_first = p.op("sync", lambda e: e.dma_start(out=sink_sb, in_=sink), inc="tab0", by=16)
        for dst_off, nel, dt, src in (
            (O_MASKS, 1536, BF16, masks),
            (O_TABQ, 512, F32, rqc), (O_TABQ + 2048, 512, F32, rqs),
            (O_TABK, 576, F32, rkc), (O_TABK + 2304, 576, F32, rks),
        ):
            d = cv(dst_off, nel, dt)
            tab_cnt = p.op("sync", lambda e, d=d, s=src: e.dma_start(out=d, in_=s), inc="tab", by=16)

        p.op("vector", lambda e: e.memset(sm(SM_EPS, 0), 1.0e-5), inc="dve")

        def x2c(slot):
            return cv(O_R1 + slot * 8192, 2048, F32)

        def emit_xdma(rt, gate_dve):
            d = x2c(rt % 6)
            p.w("sync", "dve", gate_dve)
            return p.op("sync", lambda e, d=d, rt=rt: e.dma_start(
                out=d, in_=xs[rt * 128:(rt + 1) * 128, :]),
                inc="x%d" % (rt % 6), by=16)

        for rt0 in range(6):
            emit_xdma(rt0, 0)

        p.w("tensor", "tab0", tab_first)
        c_sinkmm = p.op("tensor", lambda e: e.matmul(
            bank_f(0, 0, 16), ones_sb, sink_sb, start=True, stop=True), inc="pe")
        p.w("vector", "pe", c_sinkmm)
        c_sinkcp = p.op("vector", lambda e: e.tensor_copy(sinkb, bank_f(0, 0, 16)), inc="dve")

        xn_done = [0] * NTH
        trevac = {}
        trpe = [0] * NTH
        a1s = [0] * NTH
        a3s = [0] * NTH
        a4s = [0] * NTH
        a5s = [0] * NTH
        a6s = [0] * NTH

        def sqb(par2):
            return cv(O_R3 + 20480 + par2 * 8192, 2048, F32)

        def xnc(slot):
            return cv(O_R1 + 49152 + slot * 4096, 2048, BF16)

        def a_sums(t):
            slot, p4 = t % 6, t % 4
            x2 = x2c(slot)
            p.w("vector", "x%d" % slot, 16 * (t // 6 + 1))
            if t >= 4:
                p.w("vector", "act", a3s[t - 4])
            a1s[t] = p.op("vector", lambda e, x2=x2, p4=p4: e.reduce_sum(
                sm(SM_S1, p4), x2, axis=AX.X), inc="dve")
            p.w("scalar", "x%d" % slot, 16 * (t // 6 + 1))
            if t >= 4:
                p.w("scalar", "dve", a4s[t - 4])
            p.op("scalar", lambda e, x2=x2, p4=p4, b2=t % 2: e.activation(
                sqb(b2)[:], x2, AF.Square, accum_out=sm(SM_S2, p4)), inc="act")
            p.w("scalar", "dve", a1s[t])
            p.op("scalar", lambda e, p4=p4: e.activation(
                sm(SM_M, p4), sm(SM_S1, p4), AF.Copy, scale=1.0 / 2048.0), inc="act")
            a3s[t] = p.op("scalar", lambda e, p4=p4: e.activation(
                sm(SM_MSQ, p4), sm(SM_S1, p4), AF.Square, scale=1.0 / 2048.0), inc="act")

        def a_var(t):
            p4 = t % 4
            p.w("vector", "act", a3s[t])
            if t >= 4:
                p.w("vector", "act", a5s[t - 4])
            a4s[t] = p.op("vector", lambda e, p4=p4: e.tensor_scalar(
                sm(SM_VAR, p4), sm(SM_S2, p4), 1.0 / 2048.0, sm(SM_MSQ, p4),
                ALU.mult, ALU.subtract), inc="dve")
            p.w("scalar", "dve", a4s[t])
            a5s[t] = p.op("scalar", lambda e, p4=p4: e.activation(
                sm(SM_LNV, p4), sm(SM_VAR, p4), AF.Ln, bias=sm(SM_EPS, 0)), inc="act")
            p.w("scalar", "act", a5s[t])
            a6s[t] = p.op("scalar", lambda e, p4=p4: e.activation(
                sm(SM_RSTD, p4), sm(SM_LNV, p4), AF.Exp, scale=-0.5), inc="act")

        def a_xn(t):
            p4 = t % 4
            p.w("vector", "act", a6s[t])
            if t >= 4:
                p.w("vector", "pe", trpe[t - 4])
            xn_done[t] = p.op("vector", lambda e, x2=x2c(t % 6), slot=t % 4, p4=p4: e.tensor_scalar(
                xnc(slot)[:], x2, sm(SM_M, p4), sm(SM_RSTD, p4),
                ALU.subtract, ALU.mult), inc="dve")

        def a_trs(t):
            slot = t % 4
            p.set_phase("A_tr")
            for g in range(4):
                gi = t * 4 + g
                p.w("tensor", "dve", xn_done[t])
                if gi == 0:
                    p.w("tensor", "dve", c_sinkcp)
                if gi >= 2:
                    ev_eng, ev_cnt = trevac[gi - 2]
                    p.w("tensor", ev_eng, ev_cnt)
                base_bank = 4 * (g % 2)
                last = None
                for j in range(4):
                    ht = 4 * g + j
                    last = p.op("tensor", lambda e, slot=slot, ht=ht, b=base_bank + j: e.transpose(
                        bank_b(b, 0, 128), xnc(slot)[:, ht * 128:(ht + 1) * 128], ident_sb),
                        inc="pe" if j == 3 else None)
                trpe[t] = last
                psrc = pall.bitcast(BF16).rearrange("q (b j) -> q b j", b=8)[
                    :, base_bank:base_bank + 4, 0:128]
                pdst = xnT3[:, 4 * g:4 * g + 4, t * 128:(t + 1) * 128]
                if g != 3:
                    p.w("scalar", "pe", last)
                    trevac[gi] = ("act", p.op("scalar", lambda e, pd=pdst, ps=psrc: e.copy(
                        pd, ps), inc="act"))
                else:
                    p.w("vector", "pe", last)
                    trevac[gi] = ("dve", p.op("vector", lambda e, pd=pdst, ps=psrc: e.tensor_copy(
                        pd, ps), inc="dve"))
            p.set_phase("A")

        for t in range(NTH + 3):
            if 6 <= t < NTH:
                emit_xdma(t, xn_done[t - 6])
            if t < NTH:
                a_sums(t)
            if 2 <= t < NTH + 2:
                a_var(t - 2)
            if 3 <= t < NTH + 3:
                a_xn(t - 3)
                a_trs(t - 3)

        trevac_act_last = trevac[34][1]
        trevac_dve_last = trevac[35][1]

        p.w("sync", "dve", xn_done[NTH - 1])
        p.w("sync", "pe", trpe[NTH - 1])
        xo_cnt = 0
        for m in range(NTR):
            xo_cnt = p.op("sync", lambda e, m=m: e.dma_start(
                out=out_sb[:, m * 2048:(m + 1) * 2048],
                in_=xs[64 + m * 128: 64 + (m + 1) * 128, :]), inc="xo", by=16)

        p.set_phase("E")
        def wgu_slot(s):
            return cv(O_R2 + s * 8192, 4096, BF16)

        def wdn_slot(s):
            return cv(O_R3 + s * 4096, 2048, BF16)

        def hT_tile(t):
            return cv(O_R3 + 20480 + t * 2048, 1024, BF16)

        def sg_c(par):
            return cv(O_R4 + par * 4096, 1024, F32)

        gup_gate_done = [0] * NI
        gup_done = [0] * NI
        silu_done = [0] * NI
        hmul_done = [0] * NI
        dn_blk = {}
        dn_evac = {}
        dn_grp_done = [0] * (NI // G)

        wgu_cnt = [0, 0, 0]
        wgu_wait = {}

        def wgu_slot_of(i):
            return (i + 2) % 3

        def emit_wgu_dma(i, gate_pe):
            s = wgu_slot_of(i)
            d = wgu_slot(s).rearrange("q (h j) -> q h j", h=16)
            srcap = wgu[:, i * 256:(i + 1) * 256].rearrange("(h q) j -> q h j", q=128)
            p.w("gpsimd", "pe", gate_pe)
            wgu_cnt[s] += 1
            wgu_wait[i] = (s, 16 * wgu_cnt[s])
            return p.op("gpsimd", lambda e, d=d, s2=srcap: e.dma_start(out=d, in_=s2),
                        inc="wgu%d" % s, by=16)

        def emit_wdn_dma(i, gate_pe):
            d = wdn_slot(i % 5)
            src = wdn[i * 128:(i + 1) * 128, :]
            p.w("gpsimd", "pe", gate_pe)
            return p.op("gpsimd", lambda e, d=d, s=src: e.dma_start(out=d, in_=s),
                        inc="wdn%d" % (i % 5), by=16)

        p.w("gpsimd", "x1", 16)
        emit_wgu_dma(0, 0)
        p.w("gpsimd", "dve", xn_done[0])
        emit_wgu_dma(1, 0)
        p.w("gpsimd", "dve", xn_done[1])
        emit_wgu_dma(2, 0)
        p.w("gpsimd", "dve", xn_done[2])
        emit_wdn_dma(0, 0)
        p.w("gpsimd", "dve", xn_done[3])
        emit_wdn_dma(1, 0)
        p.w("gpsimd", "dve", xn_done[4])
        for i in range(2, 5):
            emit_wdn_dma(i, 0)

        def emit_dn_block(gsrc, bl):
            bi = gsrc * 16 + bl
            msub, half = bl // 2, bl % 2
            pairbase = 4 if (bl % 2 == 0) else 6
            if bl == 0:
                for it in range(4 * gsrc, 4 * gsrc + 4):
                    p.w("tensor", "wdn%d" % (it % 5), 16 * (it // 5 + 1))
            if bl == 0:
                p.w("tensor", "dve", hmul_done[4 * gsrc + 3])
            if gsrc == 0 and bl < 2:
                p.w("tensor", "dve", trevac_dve_last)
            if bi >= 2:
                p.w("tensor", "dve", dn_evac[bi - 2])
            if bi == 0:
                p.w("vector", "xo", xo_cnt)
            lastmm = None
            for ii in range(4):
                it = 4 * gsrc + ii
                lhsT = hT_tile((gsrc % 2) * 4 + ii)[:, msub * 128:(msub + 1) * 128]
                rhs = wdn_slot(it % 5)
                for ch in range(2):
                    lastmm = p.op("tensor", lambda e, l=lhsT, r=rhs, half=half, ch=ch,
                                  pb=pairbase, ii=ii: e.matmul(
                        bank_f(pb + ch), l, r[:, half * 1024 + ch * 512: half * 1024 + (ch + 1) * 512],
                        start=(ii == 0), stop=(ii == 3)),
                        inc="pe" if (ii == 3 and ch == 1) else None)
            dn_blk[bi] = lastmm
            p.w("vector", "pe", lastmm)
            psrc = bank_f(pairbase, 0, 1024)
            dsl = out_sb[:, msub * 2048 + half * 1024: msub * 2048 + (half + 1) * 1024]
            dn_evac[bi] = p.op("vector", lambda e, d=dsl, s=psrc: e.tensor_tensor(
                d, d, s, ALU.add), inc="dve")

        for i in range(NI):
            g = i // 4
            p.w("tensor", "wgu%d" % wgu_wait[i][0], wgu_wait[i][1])
            if i == 0:
                p.w("tensor", "act", trevac_act_last)
                p.w("tensor", "dve", trevac_dve_last)
            if i > 0:
                p.w("tensor", "act", silu_done[i - 1])
            slot = wgu_slot(wgu_slot_of(i))
            for ht in range(KT):
                g_st = slot[:, ht * 256: ht * 256 + 128]
                mv0 = xnT[:, ht * 1152 + 64: ht * 1152 + 64 + 512]
                mv1 = xnT[:, ht * 1152 + 64 + 512: ht * 1152 + 64 + 1024]
                p.op("tensor", lambda e, s=g_st, m=mv0, ht=ht: e.matmul(
                    bank_f(0), s, m, start=(ht == 0), stop=(ht == 15)), inc=None)
                gup_gate_done[i] = p.op("tensor", lambda e, s=g_st, m=mv1, ht=ht: e.matmul(
                    bank_f(1), s, m, start=(ht == 0), stop=(ht == 15)),
                    inc="pe" if ht == 15 else None) or gup_gate_done[i]
            if i > 0:
                p.w("tensor", "dve", hmul_done[i - 1])
            for ht in range(KT):
                u_st = slot[:, ht * 256 + 128: ht * 256 + 256]
                mv0 = xnT[:, ht * 1152 + 64: ht * 1152 + 64 + 512]
                mv1 = xnT[:, ht * 1152 + 64 + 512: ht * 1152 + 64 + 1024]
                p.op("tensor", lambda e, s=u_st, m=mv0, ht=ht: e.matmul(
                    bank_f(2), s, m, start=(ht == 0), stop=(ht == 15)), inc=None)
                gup_done[i] = p.op("tensor", lambda e, s=u_st, m=mv1, ht=ht: e.matmul(
                    bank_f(3), s, m, start=(ht == 0), stop=(ht == 15)),
                    inc="pe" if ht == 15 else None) or gup_done[i]
            if g >= 1 and i % 4 == 0:
                for bl in range(16):
                    emit_dn_block(g - 1, bl)
                dn_grp_done[g - 1] = dn_blk[(g - 1) * 16 + 15]
            if i >= 4 and i % 4 == 0:
                for t in range(i + 1, min(i + 5, NI)):
                    emit_wdn_dma(t, dn_grp_done[i // 4 - 1])
            p.w("scalar", "pe", gup_gate_done[i])
            if i >= 2:
                p.w("scalar", "dve", hmul_done[i - 2])
            silu_done[i] = p.op("scalar", lambda e, par=i % 2: e.activation(
                sg_c(par)[:], bank_f(0, 0, 1024), AF.Silu), inc="act")
            p.w("vector", "act", silu_done[i])
            p.w("vector", "pe", gup_done[i])
            if g >= 2:
                p.w("vector", "pe", dn_blk[(g - 2) * 16 + 15])
            ht_dst = hT_tile((g % 2) * 4 + i % 4)
            hmul_done[i] = p.op("vector", lambda e, d=ht_dst, par=i % 2: e.tensor_tensor(
                d[:], sg_c(par)[:], bank_f(2, 0, 1024), ALU.mult), inc="dve")
            if i + 3 < NI:
                emit_wgu_dma(i + 3, gup_done[i])
        for bl in range(16):
            emit_dn_block(NI // 4 - 1, bl)
        dn_grp_done[NI // 4 - 1] = dn_blk[(NI // 4 - 1) * 16 + 15]
        ffn_dn_end_dve = dn_evac[(NI // 4 - 1) * 16 + 15]

        p.set_phase("B")
        ffn_pe_end = dn_grp_done[NI // 4 - 1]
        wvk = cv(O_R2, 8192, BF16)
        wvk3 = wvk.rearrange("q (h j) -> q h j", h=16)
        wkA = cv(O_WQ1, 4096, BF16)
        wkB = cv(O_R4 + 9216, 4096, BF16)
        p.w("gpsimd", "pe", gup_done[16])
        p.op("gpsimd", lambda e: e.dma_start(
            out=wkA.rearrange("q (h j) -> q h j", h=8),
            in_=wk[0:1024, :].rearrange("(h q) j -> q h j", q=128)), inc="wk", by=16)
        p.w("gpsimd", "pe", gup_done[17])
        p.op("gpsimd", lambda e: e.dma_start(
            out=wkB.rearrange("q (h j) -> q h j", h=8),
            in_=wk[1024:2048, :].rearrange("(h q) j -> q h j", q=128)), inc="wk1", by=16)
        p.w("gpsimd", "pe", gup_done[61])
        p.op("gpsimd", lambda e: e.dma_start(
            out=wvk3[:, 0:8, :],
            in_=wv[0:1024, :].rearrange("(h q) j -> q h j", q=128)), inc="wv", by=16)
        p.w("gpsimd", "pe", gup_done[62])
        p.op("gpsimd", lambda e: e.dma_start(
            out=wvk3[:, 8:16, :],
            in_=wv[1024:2048, :].rearrange("(h q) j -> q h j", q=128)), inc="wv1", by=16)

        rkc_sb = cv(O_TABK, 576, F32)
        rks_sb = cv(O_TABK + 2304, 576, F32)
        scr_a = cv(O_SCR, 256, F32)
        scr_b = cv(O_SCR + 1024, 256, F32)
        scr_c = cv(O_SCR2, 256, F32)
        scr_d = cv(O_SCR2 + 1024, 256, F32)

        def natc(par):
            return cv(O_NAT + par * 1024, 512, BF16)

        kmm = [0] * NTH
        krope = [0] * NTH
        ktr2 = {}
        ktrevac = {}

        def k_mms(rt):
            par = rt % 2
            p.w("tensor", "wk", 16)
            p.w("tensor", "wk1", 16)
            if rt >= 2:
                p.w("tensor", "dve", krope[rt - 2])
            else:
                p.w("tensor", "dve", ffn_dn_end_dve)
                p.w("tensor", "act", silu_done[NI - 1])
            for ht in range(KT):
                wsl = wkA[:, ht * 512:(ht + 1) * 512] if ht < 8 else \
                    wkB[:, (ht - 8) * 512:(ht - 7) * 512]
                kmm[rt] = p.op("tensor", lambda e, ht=ht, rt=rt, par=par, w=wsl: e.matmul(
                    bank_f(par), xnT[:, ht * 1152 + rt * 128: ht * 1152 + (rt + 1) * 128],
                    w, start=(ht == 0), stop=(ht == 15)),
                    inc="pe" if ht == 15 else None) or kmm[rt]

        def k_rope(rt):
            par = rt % 2
            if rt == 0:
                p.w("vector", "tab", tab_cnt)
            p.w("vector", "pe", kmm[rt])
            if rt >= 2:
                p.w("vector", "pe", ktr2[(rt - 2) * 4 + 3])
            ps3 = bank_f(par).rearrange("q (h d) -> q h d", h=4)
            r3, i3 = ps3[:, :, 0:64], ps3[:, :, 64:128]
            cos3 = rkc_sb[:, rt * 64:(rt + 1) * 64][:, None, :].broadcast_to((128, 4, 64))
            sin3 = rks_sb[:, rt * 64:(rt + 1) * 64][:, None, :].broadcast_to((128, 4, 64))
            sa3 = scr_a.rearrange("q (h d) -> q h d", h=4)
            sb3 = scr_b.rearrange("q (h d) -> q h d", h=4)
            sc3 = scr_c.rearrange("q (h d) -> q h d", h=4)
            sd3 = scr_d.rearrange("q (h d) -> q h d", h=4)
            n3 = natc(par).rearrange("q (h d) -> q h d", h=4)
            p.fence("vector")
            p.op("vector", lambda e, a=sa3, r=r3, c=cos3: e.tensor_tensor(a, r, c, ALU.mult), inc="dve")
            p.op("vector", lambda e, b=sb3, i=i3, s=sin3: e.tensor_tensor(b, i, s, ALU.mult), inc="dve")
            p.op("vector", lambda e, a=sc3, r=r3, s=sin3: e.tensor_tensor(a, r, s, ALU.mult), inc="dve")
            p.op("vector", lambda e, b=sd3, i=i3, c=cos3: e.tensor_tensor(b, i, c, ALU.mult), inc="dve")
            p.fence("vector")
            p.op("vector", lambda e, o=n3[:, :, 0:64], a=sa3, b=sb3: e.tensor_tensor(
                o, a, b, ALU.subtract), inc="dve")
            krope[rt] = p.op("vector", lambda e, o=n3[:, :, 64:128], a=sc3, b=sd3: e.tensor_tensor(
                o, a, b, ALU.add), inc="dve")

        def k_trs(rt):
            par = rt % 2
            p.w("tensor", "dve", krope[rt])
            for kv in range(4):
                kti = rt * 4 + kv
                if kti >= 2:
                    p.w("tensor", "act", ktrevac[kti - 2])
                else:
                    p.w("tensor", "dve", ffn_dn_end_dve)
                b = 4 + (kv % 2)
                ktr2[kti] = p.op("tensor", lambda e, par=par, kv=kv, b=b: e.transpose(
                    bank_b(b, 0, 128), natc(par)[:, kv * 128:(kv + 1) * 128], ident_sb), inc="pe")
                p.w("scalar", "pe", ktr2[kti])
                ktrevac[kti] = p.op("scalar", lambda e, kv=kv, rt=rt, b=b: e.copy(
                    kTb[:, kv * 1152 + rt * 128: kv * 1152 + (rt + 1) * 128],
                    bank_b(b, 0, 128)), inc="act")

        for rt in range(NTH + 2):
            if rt < NTH:
                k_mms(rt)
            if rt >= 2:
                k_trs(rt - 2)
            if rt < NTH:
                k_rope(rt)

        vmm = [0] * NTH
        vevac = [0] * NTH
        for rt in range(NTH):
            par = rt % 2
            p.w("tensor", "wv", 16)
            p.w("tensor", "wv1", 16)
            if rt >= 2:
                p.w("tensor", "dve", vevac[rt - 2])
            else:
                p.w("tensor", "dve", krope[8 - rt])
            for ht in range(KT):
                vmm[rt] = p.op("tensor", lambda e, ht=ht, rt=rt, par=par: e.matmul(
                    bank_f(par), xnT[:, ht * 1152 + rt * 128: ht * 1152 + (rt + 1) * 128],
                    wvk[:, ht * 512:(ht + 1) * 512],
                    start=(ht == 0), stop=(ht == 15)),
                    inc="pe" if ht == 15 else None) or vmm[rt]
            p.w("vector", "pe", vmm[rt])
            vevac[rt] = p.op("vector", lambda e, rt=rt, par=par: e.tensor_copy(
                v_sb[:, rt * 512:(rt + 1) * 512], bank_f(par)), inc="dve")

        rqc_sb = cv(O_TABQ, 512, F32)
        rqs_sb = cv(O_TABQ + 2048, 512, F32)

        WQ_STAGE = {
            (1, 0): (O_R2 + 16384, "wq2", 16),
            (1, 1): (O_WQ1, "wq3", 16),
            (0, 0): (O_R3 + 24576, "wqa", 16),
            (0, 1): (O_R3 + 16384, "wqb", 16),
            (2, 0): (O_R2, "wq0", 16),
            (2, 1): (O_R2 + 8192, "wq1", 16),
            (3, 0): (O_R2 + 16384, "wq2", 32),
            (3, 1): (O_WQ1, "wq3", 32),
        }

        def wq_home(dc4, half):
            return cv(WQ_STAGE[(dc4, half)][0], 4096, BF16)

        qmm = {}
        qrope = {}
        qtr2 = {}
        qtrevac = {}
        qT_ready = [0] * NH

        def emit_wq_dma(dc4, half, gate_pe):
            off, sem, _ = WQ_STAGE[(dc4, half)]
            d = wq_home(dc4, half).rearrange("q (h j) -> q h j", h=8)
            srcap = wq[half * 1024:(half + 1) * 1024,
                       dc4 * 512:(dc4 + 1) * 512].rearrange("(h q) j -> q h j", q=128)
            p.w("gpsimd", "pe", gate_pe)
            return p.op("gpsimd", lambda e, d=d, s2=srcap: e.dma_start(out=d, in_=s2),
                        inc=sem, by=16)

        emit_wq_dma(1, 0, gup_done[NI - 1])
        emit_wq_dma(0, 0, ffn_pe_end)
        emit_wq_dma(0, 1, ffn_pe_end)
        emit_wq_dma(1, 1, kmm[NTH - 1])
        emit_wq_dma(2, 0, vmm[NTH - 1])
        emit_wq_dma(2, 1, vmm[NTH - 1])
        DC4ORDER = [1, 0, 2, 3]

        def q_mms(idx):
            dc4, mt = DC4ORDER[idx // 8], idx % 8
            par = idx % 4
            for half in range(2):
                _, sem, cnt = WQ_STAGE[(dc4, half)]
                p.w("tensor", sem, cnt)
            if idx >= 4:
                p.w("tensor", "dve", qrope[idx - 4])
            else:
                p.w("tensor", "dve", vevac[NTH - 1])
            for half in range(2):
                slot = wq_home(dc4, half)
                for ht8 in range(8):
                    ht = half * 8 + ht8
                    qmm[idx] = p.op("tensor", lambda e, ht=ht, ht8=ht8, mt=mt, par=par, s=slot: e.matmul(
                        bank_f(par),
                        xnT[:, ht * 1152 + 64 + mt * 128: ht * 1152 + 64 + (mt + 1) * 128],
                        s[:, ht8 * 512:(ht8 + 1) * 512],
                        start=(ht == 0), stop=(ht == 15)),
                        inc="pe" if ht == 15 else None) or qmm.get(idx, 0)

        def q_rope(idx):
            mt = idx % 8
            par = idx % 4
            p.w("vector", "pe", qmm[idx])
            if idx >= 2:
                p.w("vector", "pe", qtr2[(idx - 2) * 4 + 3])
            else:
                p.w("vector", "pe", ktr2[(8 - idx) * 4 + 3])
            ps3 = bank_f(par).rearrange("q (h d) -> q h d", h=4)
            r3, i3 = ps3[:, :, 0:64], ps3[:, :, 64:128]
            cos3 = rqc_sb[:, mt * 64:(mt + 1) * 64][:, None, :].broadcast_to((128, 4, 64))
            sin3 = rqs_sb[:, mt * 64:(mt + 1) * 64][:, None, :].broadcast_to((128, 4, 64))
            sa3 = scr_a.rearrange("q (h d) -> q h d", h=4)
            sb3 = scr_b.rearrange("q (h d) -> q h d", h=4)
            sc3 = scr_c.rearrange("q (h d) -> q h d", h=4)
            sd3 = scr_d.rearrange("q (h d) -> q h d", h=4)
            n3 = natc(idx % 2).rearrange("q (h d) -> q h d", h=4)
            p.fence("vector")
            p.op("vector", lambda e, a=sa3, r=r3, c=cos3: e.tensor_tensor(a, r, c, ALU.mult), inc="dve")
            p.op("vector", lambda e, b=sb3, i=i3, s=sin3: e.tensor_tensor(b, i, s, ALU.mult), inc="dve")
            p.op("vector", lambda e, a=sc3, r=r3, s=sin3: e.tensor_tensor(a, r, s, ALU.mult), inc="dve")
            p.op("vector", lambda e, b=sd3, i=i3, c=cos3: e.tensor_tensor(b, i, c, ALU.mult), inc="dve")
            p.fence("vector")
            p.op("vector", lambda e, o=n3[:, :, 0:64], a=sa3, b=sb3: e.tensor_tensor(
                o, a, b, ALU.subtract), inc="dve")
            qrope[idx] = p.op("vector", lambda e, o=n3[:, :, 64:128], a=sc3, b=sd3:
                              e.tensor_tensor(o, a, b, ALU.add), inc="dve")

        def q_trs(idx):
            dc4, mt = DC4ORDER[idx // 8], idx % 8
            p.w("tensor", "dve", qrope[idx])
            for j in range(4):
                tri = idx * 4 + j
                h = dc4 * 4 + j
                if tri >= 2:
                    p.w("tensor", "act", qtrevac[tri - 2])
                else:
                    p.w("tensor", "act", ktrevac[34 + tri])
                b = 4 + (j % 2)
                qtr2[tri] = p.op("tensor", lambda e, par2=idx % 2, j=j, b=b: e.transpose(
                    bank_b(b, 0, 128), natc(par2)[:, j * 128:(j + 1) * 128], ident_sb),
                    inc="pe")
                p.w("scalar", "pe", qtr2[tri])
                qtrevac[tri] = p.op("scalar", lambda e, h=h, mt=mt, b=b: e.copy(
                    qT[:, h * 1024 + mt * 128: h * 1024 + (mt + 1) * 128],
                    bank_b(b, 0, 128)), inc="act")
                qT_ready[h] = qtrevac[tri]

        for idx in range(32 + 2):
            if idx < 32:
                q_mms(idx)
            if idx >= 2:
                q_trs(idx - 2)
            if idx < 32:
                q_rope(idx)
            if idx == 7:
                emit_wq_dma(3, 0, qmm[idx])
                emit_wq_dma(3, 1, qmm[idx])

        q_end_pe = qmm[31]

        def wo_tile(dt):
            if dt <= 8:
                off = O_XNT + dt * 4096
            elif dt <= 14:
                off = O_R2 + (dt - 9) * 4096
            else:
                off = O_R3 + 32768
            return cv(off, 2048, BF16)

        for dt in [15, 9, 10, 11, 12] + list(range(9)) + [13, 14]:
            d = wo_tile(dt)
            src = wo[dt * 128:(dt + 1) * 128, :]
            eng = "gpsimd"
            if dt == 15:
                gate = ffn_pe_end
            elif 9 <= dt <= 12:
                gate = qmm[23]
            else:
                gate = q_end_pe
            p.w(eng, "pe", gate)
            p.op(eng, lambda e, d=d, s=src: e.dma_start(out=d, in_=s),
                 inc="wo%d" % (dt // 4), by=16)

        p.set_phase("C")
        def e2_c(p4):
            return cv(O_ESB + p4 * 2048, 512, F32)

        def p2_c(p4):
            return cv(O_PSB + p4 * 1024, 512, BF16)

        def pT2_c(p4):
            return cv(O_PTSB + p4 * 1024, 512, BF16)

        qTh3 = qT.rearrange("q (h c) -> q h c", h=16)

        c_s2 = {}
        c_e2 = {}
        c_p2 = {}
        c_tr2 = {}
        c_t2 = {}
        c_os2 = {}
        oevac2 = {}
        woevac_a = [0] * NTR
        woevac_b = [0] * NTR

        PSTEPS = [(qt, hp) for qt in range(NTR) for hp in range(NH // 2)]
        NP = len(PSTEPS)

        def c_smm(s):
            qt, hp = PSTEPS[s]
            h0 = 2 * hp
            kind = 0 if qt == 0 else (2 if qt == 7 else 1)
            sb = s % 2
            kvh = hp // 2
            if s >= 2:
                p.w("tensor", "act", c_e2[s - 2])
            if s == 0:
                p.w("tensor", "dve", qrope[31])
                p.w("tensor", "tab", tab_cnt)
            if qt == 0:
                p.w("tensor", "act", qT_ready[h0 + 1])
                p.w("tensor", "act", qT_ready[h0])
            for hh in range(2):
                p.op("tensor", lambda e, h=h0 + hh, qt=qt, sb=sb, kvh=kvh, hh=hh: e.matmul(
                    bank_f(sb, hh * 256, 256),
                    qT[:, h * 1024 + qt * 128: h * 1024 + (qt + 1) * 128],
                    kTb[:, kvh * 1152 + qt * 128: kvh * 1152 + qt * 128 + 256],
                    start=(hh == 0), stop=False))
            for hh in range(2):
                c_s2[s] = p.op("tensor", lambda e, sb=sb, kind=kind, hh=hh: e.matmul(
                    bank_f(sb, hh * 256, 256), ident_sb,
                    masks_sb[:, kind * 512 + hh * 256: kind * 512 + (hh + 1) * 256],
                    start=False, stop=(hh == 1)), inc="pe" if hh == 1 else None) or c_s2.get(s)

        def c_exp(s):
            qt, hp = PSTEPS[s]
            h0 = 2 * hp
            sb = s % 2
            p4 = s % 4
            p.w("scalar", "pe", c_s2[s])
            if s >= 4:
                p.w("scalar", "dve", c_p2[s - 4])
            for hh in range(2):
                c_e2[s] = p.op("scalar", lambda e, sb=sb, p4=p4, hh=hh, h=h0 + hh: e.activation(
                    e2_c(p4)[:, hh * 256:(hh + 1) * 256],
                    bank_f(sb, hh * 256, 256), AF.Exp,
                    bias=sinkb[:, h:h + 1],
                    accum_out=sm(SM_DEN, 2 * p4 + hh)), inc="act")

        def c_recip_pmult(s):
            p4 = s % 4
            p.w("vector", "act", c_e2[s])
            p.op("vector", lambda e, p4=p4: e.reciprocal(
                small[:, SM_RC + 2 * p4: SM_RC + 2 * p4 + 2],
                small[:, SM_DEN + 2 * p4: SM_DEN + 2 * p4 + 2]), inc="dve")
            p.fence("vector")
            if s >= 4:
                p.w("vector", "pe", c_tr2[s - 4])
            for hh in range(2):
                c_p2[s] = p.op("vector", lambda e, p4=p4, hh=hh: e.tensor_scalar(
                    p2_c(p4)[:, hh * 256:(hh + 1) * 256],
                    e2_c(p4)[:, hh * 256:(hh + 1) * 256],
                    sm(SM_RC, 2 * p4 + hh), None, ALU.mult), inc="dve")

        def c_tr_st(s):
            p4 = s % 4
            tb = 6 + s % 2
            p.w("tensor", "dve", c_p2[s])
            if s >= 2:
                p.w("tensor", "act", c_t2[s - 2])
            last = None
            for tj in range(4):
                jj, hh = tj // 2, tj % 2
                src = p2_c(p4)[:, hh * 256 + jj * 128: hh * 256 + (jj + 1) * 128]
                last = p.op("tensor", lambda e, tb=tb, tj=tj, src=src: e.transpose(
                    bank_b(tb, tj * 128, 128), src, ident_sb),
                    inc="pe" if tj == 3 else None) or last
            c_tr2[s] = last

        def c_ptevac(s):
            p4 = s % 4
            tb = 6 + s % 2
            p.w("scalar", "pe", c_tr2[s])
            if s >= 4:
                p.w("scalar", "pe", c_os2[s - 4])
            c_t2[s] = p.op("scalar", lambda e, p4=p4, tb=tb: e.copy(
                pT2_c(p4)[:], bank_b(tb, 0, 512)), inc="act")

        def c_o(s):
            qt, hp = PSTEPS[s]
            p4 = s % 4
            ob = 2 + s % 2
            kvh = hp // 2
            p.w("tensor", "act", c_t2[s])
            if s >= 2:
                p.w("tensor", "act", oevac2[s - 2])
            co = None
            for j in range(2):
                lhsT = v_sb[:, (qt + j) * 512 + kvh * 128: (qt + j) * 512 + (kvh + 1) * 128]
                co = p.op("tensor", lambda e, l=lhsT, ob=ob, p4=p4, j=j: e.matmul(
                    bank_f(ob, 0, 256), l, pT2_c(p4)[:, j * 256:(j + 1) * 256],
                    start=(j == 0), stop=(j == 1)), inc="pe" if j == 1 else None) or co
            c_os2[s] = co

        def c_oevac(s):
            qt, hp = PSTEPS[s]
            ob = 2 + s % 2
            h0 = 2 * hp
            p.w("scalar", "pe", c_os2[s])
            psrc = bank_f(ob, 0, 256).rearrange("q (h c) -> q h c", h=2)
            pdst = qTh3[:, h0:h0 + 2, qt * 128:(qt + 1) * 128]
            oevac2[s] = p.op("scalar", lambda e, pd=pdst, ps=psrc: e.copy(pd, ps),
                             inc="act")

        def c_wo_unit(m, u):
            pass_, dt = u // 16, u % 16
            p.w("tensor", "wo%d" % (dt // 4), 64)
            p.w("tensor", "act", oevac2[m * 8 + dt // 2])
            if u == 0:
                if m == 0:
                    p.w("tensor", "act", qtrevac[127])
                else:
                    p.w("tensor", "dve", woevac_b[m - 1])
            if u == 16:
                p.w("tensor", "dve", woevac_a[m])
            last = None
            for chh in range(2):
                col = (pass_ * 2 + chh) * 512
                last = p.op("tensor", lambda e, dt=dt, m=m, chh=chh, col=col: e.matmul(
                    bank_f(4 + chh),
                    qT[:, dt * 1024 + m * 128: dt * 1024 + (m + 1) * 128],
                    wo_tile(dt)[:, col:col + 512],
                    start=(dt == 0), stop=(dt == 15)),
                    inc="pe" if chh == 1 else None) or last
            if u == 15:
                p.w("vector", "pe", last)
                woevac_a[m] = p.op("vector", lambda e, m=m: e.tensor_tensor(
                    out_sb[:, m * 2048: m * 2048 + 1024],
                    out_sb[:, m * 2048: m * 2048 + 1024],
                    bank_f(4, 0, 1024), ALU.add), inc="dve")
            if u == 31:
                p.w("vector", "pe", last)
                woevac_b[m] = p.op("vector", lambda e, m=m: e.tensor_tensor(
                    out_sb[:, m * 2048 + 1024: (m + 1) * 2048],
                    out_sb[:, m * 2048 + 1024: (m + 1) * 2048],
                    bank_f(4, 0, 1024), ALU.add), inc="dve")

        for slot in range(NP + 10):
            if 4 <= slot < NP + 4:
                c_ptevac(slot - 4)
            if 6 <= slot < NP + 6:
                c_oevac(slot - 6)
            if slot < NP:
                c_smm(slot)
            if 5 <= slot < NP + 5:
                c_o(slot - 5)
            if 3 <= slot < NP + 3:
                c_tr_st(slot - 3)
            if slot >= 10:
                m_u, jj = (slot - 2) // 8 - 1, (slot - 2) % 8
                if m_u < NTR:
                    for du in range(4):
                        c_wo_unit(m_u, 4 * jj + du)
            if 1 <= slot < NP + 1:
                c_exp(slot - 1)
            if 2 <= slot < NP + 2:
                c_recip_pmult(slot - 2)

        for mtile in range(NTR):
            p.w("sync", "dve", woevac_b[mtile])
            p.op("sync", lambda e, m=mtile: e.dma_start(
                out=out[m * 128:(m + 1) * 128, :],
                in_=out_sb[:, m * 2048:(m + 1) * 2048]), inc="st", by=16)
        p.w("sync", "st", 16 * NTR)


        def replay(eng, items):
            for it in items:
                if it[0] == "w":
                    eng.wait_ge(sems[it[1]], it[2])
                else:
                    inst = it[1](eng)
                    if it[2] is not None:
                        inst.then_inc(sems[it[2]], it[3])

        @block.tensor
        def _(t):
            replay(t, p.streams["tensor"])

        @block.vector
        def _(v):
            replay(v, p.streams["vector"])

        @block.scalar
        def _(s):
            replay(s, p.streams["scalar"])

        @block.sync
        def _(s):
            replay(s, p.streams["sync"])

        @block.gpsimd
        def _(g):
            replay(g, p.streams["gpsimd"])

    return nc


def _host_inputs(L0_attn_x, L0_attn_wq, L0_attn_wk, L0_attn_wv, L0_attn_wo,
                 L0_attn_sink, L0_ffn_w_gate_up, L0_ffn_w_down):
    xf = np.asarray(L0_attn_x, dtype=np.float32).reshape(T, C)
    wq = np.asarray(L0_attn_wq, dtype=np.float32)
    wk = np.asarray(L0_attn_wk, dtype=np.float32)
    wv = np.ascontiguousarray(np.asarray(L0_attn_wv, dtype=np.float32))
    wo = np.ascontiguousarray(np.asarray(L0_attn_wo, dtype=np.float32))
    sink = np.asarray(L0_attn_sink, dtype=np.float32).reshape(1, NH)
    wgu_in = np.asarray(L0_ffn_w_gate_up, dtype=np.float32)
    wdn_in = np.asarray(L0_ffn_w_down, dtype=np.float32)

    perm = np.concatenate([np.arange(0, HD, 2), np.arange(1, HD, 2)])
    idx_q = np.concatenate([h * HD + perm for h in range(NH)])
    idx_k = np.concatenate([h * HD + perm for h in range(NKV)])
    wq_p = np.ascontiguousarray(wq[:, idx_q])
    wk_p = np.ascontiguousarray(wk[:, idx_k])

    wguT = wgu_in.T
    wgu2 = np.ascontiguousarray(
        wguT.reshape(C, 2, NI, 128).transpose(0, 2, 1, 3).reshape(C, 2 * INTER))
    wdnT = np.ascontiguousarray(wdn_in.T)

    ident = np.eye(128, dtype=ml_dtypes.bfloat16)
    ones = np.ones((1, 128), dtype=np.float32)

    inv = (1.0 / (10000.0 ** (np.arange(64, dtype=np.float64) / 64.0)))

    base_mask = np.full((128, 256), NEG, dtype=np.float32)
    rr = np.arange(128)[:, None]
    jj = np.arange(256)[None, :]
    band = (jj >= rr) & (jj <= rr + 128)
    base_mask[band] = 0.0

    in_maps = []
    for c in range(NCORES):
        start = c * R
        lo, hi = start - HALO, start + R + HALO
        slab = np.zeros((HR, C), dtype=np.float32)
        s0, s1 = max(lo, 0), min(hi, T)
        slab[s0 - lo: s1 - lo] = xf[s0:s1]

        posq = (start + np.arange(R, dtype=np.float64))
        angq = posq[:, None] * inv[None, :]
        rqc = (np.cos(angq) * SCALE).astype(np.float32).reshape(NTR, 128, 64)
        rqs = (np.sin(angq) * SCALE).astype(np.float32).reshape(NTR, 128, 64)
        rqc = np.ascontiguousarray(rqc.transpose(1, 0, 2).reshape(128, NTR * 64))
        rqs = np.ascontiguousarray(rqs.transpose(1, 0, 2).reshape(128, NTR * 64))
        posk = (start - HALO + np.arange(HR, dtype=np.float64))
        angk = posk[:, None] * inv[None, :]
        rkc = np.cos(angk).astype(np.float32).reshape(NTH, 128, 64)
        rks = np.sin(angk).astype(np.float32).reshape(NTH, 128, 64)
        rkc = np.ascontiguousarray(rkc.transpose(1, 0, 2).reshape(128, NTH * 64))
        rks = np.ascontiguousarray(rks.transpose(1, 0, 2).reshape(128, NTH * 64))

        m_first = base_mask.copy()
        m_mid = base_mask.copy()
        m_last = base_mask.copy()
        if c == 0:
            m_first[:, :HALO] = NEG
        if c == NCORES - 1:
            m_last[:, 192:] = NEG
        masks = np.ascontiguousarray(np.concatenate(
            [m_first, m_first, m_mid, m_mid, m_last, m_last],
            axis=1)).astype(ml_dtypes.bfloat16)

        in_maps.append({
            "xs": slab, "wq": wq_p, "wk": wk_p, "wv": wv, "wo": wo,
            "wgu": wgu2, "wdn": wdnT, "rqc": rqc, "rqs": rqs,
            "rkc": rkc, "rks": rks, "masks": masks, "ident": ident,
            "ones": ones, "sink": sink,
        })
    return in_maps


def kernel(L0_attn_x, L0_attn_wq, L0_attn_wk, L0_attn_wv, L0_attn_wo,
           L0_attn_sink, L0_ffn_w_gate_up, L0_ffn_w_down):
    in_maps = _host_inputs(L0_attn_x, L0_attn_wq, L0_attn_wk, L0_attn_wv,
                           L0_attn_wo, L0_attn_sink, L0_ffn_w_gate_up,
                           L0_ffn_w_down)
    nc = build_program()
    res = run_bass_kernel_spmd(nc, in_maps, core_ids=list(range(NCORES)))
    outs = [np.asarray(r["out"], dtype=np.float32) for r in res.results]
    return np.concatenate(outs, axis=0).reshape(1, T, C)


if __name__ == "__main__":
    rng = np.random.default_rng(0)
    args = {
        "L0_attn_x": rng.standard_normal((1, T, C), dtype=np.float32),
        "L0_attn_wq": rng.standard_normal((C, C), dtype=np.float32) * 0.02,
        "L0_attn_wk": rng.standard_normal((C, 512), dtype=np.float32) * 0.02,
        "L0_attn_wv": rng.standard_normal((C, 512), dtype=np.float32) * 0.02,
        "L0_attn_wo": rng.standard_normal((C, C), dtype=np.float32) * 0.02,
        "L0_attn_sink": rng.standard_normal((1, NH, 1, 1)).astype(np.float32),
        "L0_ffn_w_gate_up": rng.standard_normal((2 * INTER, C), dtype=np.float32) * 0.02,
        "L0_ffn_w_down": rng.standard_normal((C, INTER), dtype=np.float32) * 0.02,
    }
    y = kernel(**args)
    print(y.shape, y.dtype)
